# version 02 (final)
# baseline (speedup 1.0000x reference)
import numpy as np
from contextlib import ExitStack
import concourse.bass as bass
import concourse.mybir as mybir
from concourse.bass import IndirectOffsetOnAxis
from concourse.bass_utils import run_bass_kernel_spmd

F32 = mybir.dt.float32
F16 = mybir.dt.float16
I32 = mybir.dt.int32
U32 = mybir.dt.uint32
AF = mybir.ActivationFunctionType
ALU = mybir.AluOpType
AX = mybir.AxisListType

B, S, D = 4, 4096, 1024
NT = 16
GROUPS, DG = 4, 256
WINDOW = [4, 16, 64, 256]
HEADS, DH = 8, 128
BLOCKS = ([0, 3, 5, 6], [1, 2, 4, 7])
KST0 = [0, 8, 16, 24]
L0_CAUSAL_W = 8

STAGE = "all"
DBG = False
DUMP2 = False
F1STAGE = 99
F1SLOT = 99


def own_tiles(p):
    return [4 * b + j for b in BLOCKS[p] for j in range(4)]


class SemC:
    def __init__(self, h):
        self.h = h
        self.n = 0

    def inc(self, instr, k=1):
        instr.then_inc(self.h, k)
        self.n += k
        return self.n


def _l1_xpos_tables():
    d, half = 128, 64
    j = np.arange(half, dtype=np.float64)
    scale = (2 * j + 0.4 * d) / (1.4 * d)
    pos = np.arange(S, dtype=np.float64)
    sc = scale[None, :] ** (pos / 512.0)[:, None]
    inv_freq = 1.0 / 10000.0 ** (j / half)
    sinu = pos[:, None] * inv_freq[None, :]
    sn, cs = np.sin(sinu), np.cos(sinu)
    mq = np.zeros(32); mk = np.zeros(32)
    for g in range(32):
        pm = 128.0 * g + 64.0
        vq = scale ** (pm / 512.0)
        vk = scale ** (-pm / 512.0)
        mq[g] = np.sqrt(vq.max() * vq.min())
        mk[g] = np.sqrt(vk.max() * vk.min())
    out = {"mq": mq, "mk": mk}
    for nm, scf, m in (("up", sc, mq), ("dn", 1.0 / sc, mk)):
        mfull = np.repeat(m, 128)[:, None]
        c = np.repeat(cs * scf, 2, axis=1) / mfull
        s_ = np.repeat(sn * scf, 2, axis=1) / mfull
        s_[:, 0::2] *= -1.0
        out["cos_" + nm] = np.ascontiguousarray(c.T.astype(np.float32))
        out["sin_" + nm] = np.ascontiguousarray(s_.T.astype(np.float32))
    return out


_L1XT = _l1_xpos_tables()
_GAMMAS = 1.0 - np.exp(np.linspace(np.log(1 / 32.), np.log(1 / 512.), 8))
KEND1 = [8, 16, 24, 32]


def _tables(p):
    t = {}
    ot = own_tiles(p)
    qpos = np.concatenate([np.arange(128 * g, 128 * (g + 1)) for g in ot]).astype(np.float64)
    t["qposbc"] = np.broadcast_to(qpos.astype(np.float32), (128, 2048)).copy()
    kt = (np.arange(32)[None, :] * 128 + np.arange(128)[:, None]).astype(np.float64)
    t["ktpos"] = kt.astype(np.float32)
    bk = np.zeros((128, 4 * 32), np.float32)
    for gi, ws in enumerate(WINDOW):
        bk[:, 32 * gi:32 * (gi + 1)] = (kt - ws).astype(np.float32)
    t["bktall"] = bk
    t["iden16"] = np.eye(128, dtype=np.float16)
    t["iden32"] = np.eye(128, dtype=np.float32)
    t["ones16"] = np.ones((128, 1), np.float16)
    t["ones1x128"] = np.ones((1, 128), np.float32)
    t["one1x1"] = np.ones((1, 1), np.float32)
    t["qposcol"] = np.ascontiguousarray(qpos.astype(np.float32).reshape(16, 128).T)
    t["tri128su"] = np.triu(np.ones((128, 128), np.float16), 1)
    t["tri16su"] = np.triu(np.ones((16, 16), np.float16), 1)
    t["tri8su"] = np.triu(np.ones((8, 8), np.float16), 1)
    t["slotbc"] = np.broadcast_to((np.arange(128) * 128.0).astype(np.float32)[:, None],
                                  (128, 8)).copy()
    t["ones1x128f16"] = np.ones((1, 128), np.float16)
    t["rowpatW"] = ((np.arange(8)[None, :] * 128) + np.arange(128)[:, None]).astype(np.float32)
    t["thr16"] = np.broadcast_to((np.arange(1, 17) * 128.0 - 127.0).astype(np.float32)[None, :],
                                 (8, 16)).copy()
    ot = own_tiles(p)
    cols = np.concatenate([np.arange(128 * g, 128 * (g + 1)) for g in ot])
    t["cosq"] = np.ascontiguousarray(_L1XT["cos_up"][:, cols])
    t["sinq"] = np.ascontiguousarray(_L1XT["sin_up"][:, cols])
    t["cosk"] = _L1XT["cos_dn"]
    t["sink"] = _L1XT["sin_dn"]
    jmod = (np.arange(512) % 128).astype(np.float64)
    gd = np.zeros((128, 8 * 1024), np.float32)
    for h in range(8):
        gd[:, 1024 * h:1024 * h + 512] = (_GAMMAS[h] ** jmod)[None, :].astype(np.float32)
        gd[:, 1024 * h + 512:1024 * (h + 1)] = (_GAMMAS[h] ** (-jmod))[None, :].astype(np.float32)
    t["gdec"] = gd
    ct = np.zeros((128, 8 * 4 * 4 * 32), np.float32)
    for h in range(8):
        for qi in range(4):
            qt0 = 4 * BLOCKS[p][qi]
            for j in range(4):
                for ktc in range(32):
                    dlt = (qt0 + j) - ktc
                    col = ((h * 4 + qi) * 4 + j) * 32 + ktc
                    if dlt >= 0:
                        ct[:, col] = np.float32((_GAMMAS[h] ** (128.0 * dlt))
                                                * _L1XT["mk"][ktc] * _L1XT["mq"][qt0 + j])
    t["ctab"] = ct
    return t


def _prep_weights(inputs):
    f32 = lambda k: inputs[k].astype(np.float32, copy=False)
    w = {}
    w["qkvwT"] = np.ascontiguousarray(f32("L0_attn_qkv_w").T)
    w["qkv_bT"] = np.ascontiguousarray(f32("L0_attn_qkv_b").reshape(24, 128).T)
    w["vbbc"] = np.broadcast_to(f32("L0_attn_qkv_b")[2048:3072][None, :], (128, 1024)).copy()
    w["offwT"] = np.ascontiguousarray(f32("L0_attn_off_w").T)
    w["lin2bc"] = np.broadcast_to(f32("L0_attn_lin2_w")[0][None, :], (128, 256)).copy()
    w["lin2bcol"] = np.broadcast_to(f32("L0_attn_lin2_b").reshape(1, 1), (128, 1)).copy()
    w["wdflat"] = np.ascontiguousarray(f32("L0_ffn_Wd").transpose(1, 0, 2).reshape(1024, 512))
    w["wgate8"] = np.ascontiguousarray(f32("L0_ffn_w_gate"))
    w["bdbc"] = np.broadcast_to(f32("L0_ffn_bd").reshape(1, 512), (128, 512)).copy()
    w["wustack"] = np.ascontiguousarray(f32("L0_ffn_Wu").reshape(512, 1024))
    w["bu8"] = np.ascontiguousarray(f32("L0_ffn_bu"))
    w["wq1"] = np.ascontiguousarray(f32("L1_attn_W_Q").reshape(1024, 128))
    w["wk1"] = np.ascontiguousarray(f32("L1_attn_W_K").reshape(1024, 128))
    w["wv1"] = np.ascontiguousarray(f32("L1_attn_W_V").reshape(1024, 128))
    w["wg_l1"] = np.ascontiguousarray(f32("L1_attn_W_G"))
    w["wo_l1"] = np.ascontiguousarray(f32("L1_attn_W_O"))
    w["gnwbc"] = np.broadcast_to(f32("L1_attn_gn_weight")[None, :], (128, 1024)).copy()
    w["gnbbc"] = np.broadcast_to(f32("L1_attn_gn_bias")[None, :], (128, 1024)).copy()
    w["router8"] = np.ascontiguousarray(f32("L1_ffn_router_w"))
    w1 = f32("L1_ffn_w1")
    w1g = w1.reshape(8, 8, 128, 8, 512).transpose(0, 3, 2, 1, 4).reshape(8 * 8 * 128, 4096)
    w["w1g"] = np.ascontiguousarray(w1g)
    w2 = f32("L1_ffn_w2")
    w2g = w2.reshape(8, 8, 4, 128, 1024).transpose(0, 1, 3, 2, 4).reshape(8 * 8 * 128, 4096)
    w["w2g"] = np.ascontiguousarray(w2g)
    w["b1f"] = np.ascontiguousarray(f32("L1_ffn_b1"))
    w["b2f"] = np.ascontiguousarray(f32("L1_ffn_b2"))
    return w


def kernel(**inputs):
    x_all = np.ascontiguousarray(inputs["L0_attn_query"].astype(np.float32, copy=False))
    w = _prep_weights(inputs)
    nc = _build(STAGE)
    in_maps = []
    for c in range(8):
        b, p = c // 2, c % 2
        m = {"x_full": np.ascontiguousarray(x_all[b]),
             "x_own": np.ascontiguousarray(
                 np.concatenate([x_all[b][512 * blk:512 * (blk + 1)] for blk in BLOCKS[p]], axis=0)),
             "pid": np.array([[p]], np.int32)}
        m.update(_tables(p))
        m.update(w)
        in_maps.append(m)
    res = run_bass_kernel_spmd(nc, in_maps, core_ids=list(range(8)))
    out = np.zeros((4, 4096, 1024), np.float32)
    for c in range(8):
        b, p = c // 2, c % 2
        y = np.asarray(res.results[c]["y"])
        for i, gt in enumerate(own_tiles(p)):
            out[b, 128 * gt:128 * (gt + 1), :] = y[128 * i:128 * (i + 1), :]
    return out


class Ctx:
    pass


def _build(stage):
    nc = bass.Bass("TRN2", target_bir_lowering=False, detect_race_conditions=False)
    cx = Ctx()
    cx.nc = nc
    T, V, SC, SY, G = nc.tensor, nc.vector, nc.scalar, nc.sync, nc.gpsimd
    cx.T, cx.V, cx.SC, cx.SY, cx.G = T, V, SC, SY, G

    def din(name, shape, dt=F32):
        return nc.dram_tensor(name, shape, dt, kind="ExternalInput").ap()

    cx.x_full = din("x_full", [S, D])
    cx.x_own = din("x_own", [2048, D])
    cx.pid = din("pid", [1, 1], I32)
    cx.qposbc = din("qposbc", [128, 2048])
    cx.ktpos_i = din("ktpos", [128, 32])
    cx.bktall = din("bktall", [128, 128])
    cx.iden16 = din("iden16", [128, 128], F16)
    cx.iden32 = din("iden32", [128, 128])
    cx.ones16_i = din("ones16", [128, 1], F16)
    cx.ones1x128_i = din("ones1x128", [1, 128])
    cx.one1x1_i = din("one1x1", [1, 1])
    cx.qkvwT = din("qkvwT", [D, 3072])
    cx.qkv_bT = din("qkv_bT", [128, 24])
    cx.vbbc_i = din("vbbc", [128, 1024])
    cx.offwT = din("offwT", [D, 256])
    cx.lin2bc_i = din("lin2bc", [128, 256])
    cx.lin2bcol_i = din("lin2bcol", [128, 1])
    cx.wdflat = din("wdflat", [1024, 512])
    cx.wgate8 = din("wgate8", [1024, 8])
    cx.bdbc_i = din("bdbc", [128, 512])
    cx.wustack = din("wustack", [512, 1024])
    cx.bu8 = din("bu8", [8, 1024])
    cx.cosq = din("cosq", [128, 2048])
    cx.sinq = din("sinq", [128, 2048])
    cx.cosk = din("cosk", [128, S])
    cx.sink = din("sink", [128, S])
    cx.gdec = din("gdec", [128, 8 * 1024])
    cx.ctab_i = din("ctab", [128, 8 * 4 * 4 * 32])
    cx.wq1 = din("wq1", [1024, 128])
    cx.wk1 = din("wk1", [1024, 128])
    cx.wv1 = din("wv1", [1024, 128])
    cx.wg_l1 = din("wg_l1", [1024, 1024])
    cx.wo_l1 = din("wo_l1", [1024, 1024])
    cx.gnwbc_i = din("gnwbc", [128, 1024])
    cx.gnbbc_i = din("gnbbc", [128, 1024])
    cx.qposcol = din("qposcol", [128, 16])
    cx.tri128su = din("tri128su", [128, 128], F16)
    cx.tri16su = din("tri16su", [16, 16], F16)
    cx.tri8su = din("tri8su", [8, 8], F16)
    cx.slotbc_i = din("slotbc", [128, 8])
    cx.ones1x128f16 = din("ones1x128f16", [1, 128], F16)
    cx.router8 = din("router8", [1024, 8])
    cx.w1g = din("w1g", [8 * 8 * 128, 4096])
    cx.w2g = din("w2g", [8 * 8 * 128, 4096])
    cx.rowpatW = din("rowpatW", [128, 8])
    cx.thr16 = din("thr16", [8, 16])
    cx.b1f = din("b1f", [8, 4096])
    cx.b2f = din("b2f", [8, 1024])

    cx.y_out = nc.dram_tensor("y", [2048, 1024], F32, kind="ExternalOutput").ap()
    cx.dbg = {}
    only = globals().get("DBG_ONLY")
    if DBG:
        for nm, sh, dt in [("dbg_dx", [128, 16], F32), ("dbg_dxrow", [1, 2048], F32),
                           ("dbg_kb", [128, 1024], F16), ("dbg_qb", [128, 1024], F16),
                           ("dbg_vb", [128, 512], F16), ("dbg_E", [128, 512], F32),
                           ("dbg_Ph", [128, 512], F16), ("dbg_dxbc", [128, 512], F32),
                           ("dbg_zrow", [1, 512], F32), ("dbg_o32", [128, 512], F32),
                           ("dbg_xf", [128, 512], F16), ("dbg_vst", [128, 1024], F32),
                           ("dbg_Ph2", [128, 512], F16), ("dbg_xin", [128, 1024], F32),
                           ("dbg_xfh0", [128, 1024], F16), ("dbg_xfl0", [128, 1024], F16), ("dbg_rz", [128, 4], F32), ("dbg_acc", [128, 128], F32),
                           ("dbg_sq", [128, 16], F32), ("dbg_mv", [128, 96], F32),
                           ("d1_x2sh", [128, 1024], F32), ("d1_keff", [128, 1024], F16),
                           ("d1_qeff", [128, 1024], F16), ("d1_vb", [128, 512], F16),
                           ("d1_Ph", [128, 512], F16), ("d1_Y", [128, 512], F32),
                           ("d1_u", [128, 128], F32), ("d1_gs", [128, 128], F32),
                           ("d1_vst", [128, 1024], F32), ("d1_post_scatter", [128, 1024], F32),
                           ("d1_qpc", [128, 16], F32), ("d1_xs0", [128, 1024], F32),
                           ("d1_x2sh_g", [128, 1024], F32),
                           ("d1_PhA", [128, 512], F16), ("d1_PhB", [128, 512], F16),
                           ("d1_PhC", [128, 512], F16), ("d1_Pl", [128, 512], F16),
                           ("d1_PlA", [128, 512], F16), ("d1_PlB", [128, 512], F16),
                           ("d1_PlC", [128, 512], F16),
                           ("dF_lg", [128, 8], F32), ("dF_wt", [128, 16], F32),
                           ("dF_eid", [128, 1], F32), ("dF_tcnt", [8, 16], F32),
                           ("dF_smal", [8, 4], F32), ("dF_dest", [128, 16], F32),
                           ("dF_oh", [128, 128], F16), ("dF_xs0", [128, 1024], F32),
                           ("dF_y0", [128, 1024], F32), ("dF_h0", [128, 512], F32),
                           ("dF_vpre", [128, 1024], F32)]:
            if only is not None and nm not in only:
                continue
            cx.dbg[nm] = nc.dram_tensor(nm, sh, dt, kind="ExternalOutput").ap()
    cx.xfm_h_d = nc.dram_tensor("xfm_h_d", [1024, S], F16, kind="Internal").ap()
    cx.xfm_l_d = nc.dram_tensor("xfm_l_d", [1024, S], F16, kind="Internal").ap()
    cx.xfmo_h_d = nc.dram_tensor("xfmo_h_d", [1024, 2048], F16, kind="Internal").ap()
    cx.xfmo_l_d = nc.dram_tensor("xfmo_l_d", [1024, 2048], F16, kind="Internal").ap()
    cx.x2sh = nc.dram_tensor("x2sh", [S, D], F32, kind="Internal", addr_space="Shared").ap()
    cx.exflag = nc.dram_tensor("exflag", [2, 1], I32, kind="Internal", addr_space="Shared").ap()
    cx.gsilu_d = nc.dram_tensor("gsilu_d", [2048, 1024], F32, kind="Internal").ap()
    cx.ufm_h_d = nc.dram_tensor("ufm_h_d", [1024, 2048], F16, kind="Internal").ap()
    cx.ufm_l_d = nc.dram_tensor("ufm_l_d", [1024, 2048], F16, kind="Internal").ap()
    cx.xsort_d = nc.dram_tensor("xsort_d", [2944, 1024], F32, kind="Internal").ap()
    cx.ysort_d = nc.dram_tensor("ysort_d", [2944, 1024], F32, kind="Internal").ap()
    cx.tb_d = nc.dram_tensor("tb_d", [16, 8], F32, kind="Internal").ap()
    cx.eid_d = nc.dram_tensor("eid_d", [128, 1], I32, kind="Internal").ap()

    cx.es = ExitStack()
    cx.LD = SemC(cx.es.enter_context(nc.semaphore("ld")))
    cx.ST = SemC(cx.es.enter_context(nc.semaphore("st")))
    cx.TS = SemC(cx.es.enter_context(nc.semaphore("ts")))
    cx.SS = SemC(cx.es.enter_context(nc.semaphore("ss")))
    cx.VS = SemC(cx.es.enter_context(nc.semaphore("vs")))
    cx.allc = [cx.LD, cx.ST, cx.TS, cx.SS, cx.VS]

    A = lambda n, sh, dt=F32: nc.alloc_sbuf_tensor(n, sh, dt).ap()
    cx.xstream = A("xstream", [128, NT * 1024])
    cx.cid16 = A("cid16", [128, 128], F16)
    cx.cid32 = A("cid32", [128, 128])
    cx.cones16 = A("cones16", [128, 1], F16)
    cx.cones1x128 = A("cones1x128", [1, 128])
    cx.cone1 = A("cone1", [1, 1])
    cx.cqpos = A("cqpos", [128, 2048])
    cx.cktpos = A("cktpos", [128, 32])
    cx.cbkt = A("cbkt", [128, 128])
    cx.cqkvbT = A("cqkvbT", [128, 24])
    cx.clin2bc = A("clin2bc", [128, 256])
    cx.clin2bcol = A("clin2bcol", [128, 1])
    cx.dxcol = A("dxcol", [128, 16])
    cx.dxrow = A("dxrow", [1, 2048])
    cx.accln = A("accln", [128, 128])
    cx.sqacc = A("sqacc", [128, 16])

    P = lambda n, sh: nc.alloc_psum_tensor(n, sh, F32).ap()
    cx.psA = P("psA", [128, 512])
    cx.psB = P("psB", [128, 512])
    cx.psO = P("psO", [128, 1024])
    cx.psR = P("psR", [1, 512])
    P16 = lambda n, sh: nc.alloc_psum_tensor(n, sh, F16).ap()
    cx.psE16 = P16("psE16", [128, 512])
    cx.psF16 = P16("psF16", [128, 512])

    def tap(nm, sbuf_slice):
        if nm in cx.dbg:
            for c in cx.allc:
                G.wait_ge(c.h, c.n)
            cx.ST.inc(G.dma_start(out=cx.dbg[nm][:], in_=sbuf_slice), 16)
            for e in (T, V, SC, SY):
                e.wait_ge(cx.ST.h, cx.ST.n)
    cx.tap = tap

    def barrier():
        for e in (T, V, SC, SY, G):
            for c in cx.allc:
                e.wait_ge(c.h, c.n)
    cx.barrier = barrier

    with nc.Block() as _blk:
        for dst, src in [(cx.cid16, cx.iden16), (cx.cid32, cx.iden32),
                         (cx.cones16, cx.ones16_i), (cx.cones1x128, cx.ones1x128_i),
                         (cx.cone1, cx.one1x1_i), (cx.cqpos, cx.qposbc),
                         (cx.cktpos, cx.ktpos_i), (cx.cbkt, cx.bktall),
                         (cx.cqkvbT, cx.qkv_bT),
                         (cx.clin2bc, cx.lin2bc_i), (cx.clin2bcol, cx.lin2bcol_i)]:
            cx.LD.inc(SY.dma_start(out=dst[:], in_=src[:]), 16)
        for t in range(NT):
            cx.LD.inc(SY.dma_start(out=cx.xstream[:, 1024 * t:1024 * (t + 1)],
                                   in_=cx.x_own[128 * t:128 * (t + 1), :]), 16)
        V.wait_ge(cx.LD.h, cx.LD.n)
        cx.VS.inc(V.memset(cx.accln[:], 0.0))
        cx.VS.inc(V.memset(cx.sqacc[:], 0.0))
        _l0(cx)
        if stage != "x1":
            _l0ffn(cx)
        if stage not in ("x1", "x2"):
            _l1attn(cx, do_ln=(stage != "x3nl"))
        if stage not in ("x1", "x2", "x3", "x3nl"):
            _l1ffn(cx)
        barrier()
        for t in range(NT):
            cx.ST.inc(G.dma_start(out=cx.y_out[128 * t:128 * (t + 1), :],
                                  in_=cx.xstream[:, 1024 * t:1024 * (t + 1)]), 16)
        if DUMP2:
            d2st = nc.dram_tensor("d2_stash", [128, 128], F32, kind="ExternalOutput").ap()
            cx.ST.inc(G.dma_start(out=d2st[:, :], in_=cx.dstash[:]), 16)
            d2eid = nc.dram_tensor("d2_eid", [128, 1], I32, kind="ExternalOutput").ap()
            d2tb = nc.dram_tensor("d2_tb", [16, 8], F32, kind="ExternalOutput").ap()
            d2xs = nc.dram_tensor("d2_xs", [256, 1024], F32, kind="ExternalOutput").ap()
            d2ys = nc.dram_tensor("d2_ys", [256, 1024], F32, kind="ExternalOutput").ap()
            stg2 = nc.alloc_sbuf_tensor("d2_stg", [128, 1024], F32).ap()
            stg2i = nc.alloc_sbuf_tensor("d2_stgi", [128, 1], I32).ap()
            stg2b = nc.alloc_sbuf_tensor("d2_stgb", [16, 8], F32).ap()
            G.wait_ge(cx.ST.h, cx.ST.n)
            cx.ST.inc(G.dma_start(out=stg2i[:], in_=cx.eid_d[:, :]), 16)
            G.wait_ge(cx.ST.h, cx.ST.n)
            cx.ST.inc(G.dma_start(out=d2eid[:, :], in_=stg2i[:]), 16)
            G.wait_ge(cx.ST.h, cx.ST.n)
            cx.ST.inc(G.dma_start(out=stg2b[:], in_=cx.tb_d[:, :]), 16)
            G.wait_ge(cx.ST.h, cx.ST.n)
            cx.ST.inc(G.dma_start(out=d2tb[:, :], in_=stg2b[:]), 16)
            for i in range(2):
                G.wait_ge(cx.ST.h, cx.ST.n)
                cx.ST.inc(G.dma_start(out=stg2[:], in_=cx.xsort_d[128 * i:128 * (i + 1), :]), 16)
                G.wait_ge(cx.ST.h, cx.ST.n)
                cx.ST.inc(G.dma_start(out=d2xs[128 * i:128 * (i + 1), :], in_=stg2[:]), 16)
                G.wait_ge(cx.ST.h, cx.ST.n)
                cx.ST.inc(G.dma_start(out=stg2[:], in_=cx.ysort_d[128 * i:128 * (i + 1), :]), 16)
                G.wait_ge(cx.ST.h, cx.ST.n)
                cx.ST.inc(G.dma_start(out=d2ys[128 * i:128 * (i + 1), :], in_=stg2[:]), 16)
        G.wait_ge(cx.ST.h, cx.ST.n)

    nc.finalize()
    return nc


def _split_transpose_tile(cx, src32, dsth, dstl, dcol, tmph, tmpl, bankmarks):
    T, V, SC = cx.T, cx.V, cx.SC
    TS, SS, VS = cx.TS, cx.SS, cx.VS
    SC.wait_ge(VS.h, VS.n)
    SS.inc(SC.copy(out=tmph[:], in_=src32[:]))
    V.wait_ge(SS.h, SS.n)
    VS.inc(V.tensor_tensor(out=tmpl[:], in0=src32[:], in1=tmph[:], op=ALU.subtract))
    T.wait_ge(SS.h, SS.n)
    T.wait_ge(VS.h, VS.n)
    for c in range(8):
        for hl, (src, dst) in enumerate(((tmph, dsth), (tmpl, dstl))):
            i = c * 2 + hl
            pt = cx.psE16 if i % 2 == 0 else cx.psF16
            T.wait_ge(VS.h, bankmarks[i % 2])
            TS.inc(T.transpose(out=pt[0:128, 0:128], in_=src[:, 128 * c:128 * (c + 1)],
                               identity=cx.cid16[:]))
            V.wait_ge(TS.h, TS.n)
            VS.inc(V.tensor_copy(out=dst[:, dcol(c):dcol(c) + 128], in_=pt[0:128, 0:128]))
            bankmarks[i % 2] = VS.n
    return bankmarks


def _l0(cx):
    nc = cx.nc
    T, V, SC, SY, G = cx.T, cx.V, cx.SC, cx.SY, cx.G
    LD, ST, TS, SS, VS = cx.LD, cx.ST, cx.TS, cx.SS, cx.VS
    barrier = cx.barrier
    psA, psB, psO, psR = cx.psA, cx.psB, cx.psO, cx.psR
    psE16, psF16 = cx.psE16, cx.psF16
    xstream = cx.xstream

    esA = ExitStack()
    SBa = lambda n, sh, dt=F16: esA.enter_context(nc.sbuf_tensor(n, sh, dt)).ap()
    xin32 = SBa("xin32", [128, 1024], F32)
    xh_t = SBa("xh_t", [128, 1024])
    xl_t = SBa("xl_t", [128, 1024])
    xfh = SBa("xfh", [128, 4096])
    xfl = SBa("xfl", [128, 4096])

    barrier()
    for sp in range(8):
        bm = [VS.n, VS.n]
        for j in range(4):
            gt = sp * 4 + j
            SY.wait_ge(VS.h, VS.n)
            LD.inc(SY.dma_start(out=xin32[:], in_=cx.x_full[128 * gt:128 * (gt + 1), :]), 16)
            SC.wait_ge(LD.h, LD.n)
            V.wait_ge(LD.h, LD.n)
            if sp == 0 and j == 0:
                cx.tap("dbg_xin", xin32[:])
            _split_transpose_tile(cx, xin32, xfh, xfl, lambda c, j=j: 512 * c + 128 * j,
                                  xh_t, xl_t, bm)
        G.wait_ge(VS.h, VS.n)
        if sp == 0:
            cx.tap("dbg_xfh0", xfh[:, 0:1024])
            cx.tap("dbg_xfl0", xfl[:, 0:1024])
        for c in range(8):
            ST.inc(G.dma_start(out=cx.xfm_h_d[128 * c:128 * (c + 1), 512 * sp:512 * (sp + 1)],
                               in_=xfh[:, 512 * c:512 * (c + 1)]), 16)
            ST.inc(G.dma_start(out=cx.xfm_l_d[128 * c:128 * (c + 1), 512 * sp:512 * (sp + 1)],
                               in_=xfl[:, 512 * c:512 * (c + 1)]), 16)
        SC.wait_ge(ST.h, ST.n)
        V.wait_ge(ST.h, ST.n)
        T.wait_ge(ST.h, ST.n)

    offh = SBa("offh", [128, 8 * 256])
    offl = SBa("offl", [128, 8 * 256])
    off32 = SBa("off32", [128, 256], F32)
    rh = SBa("rh", [128, 1024])
    rl = SBa("rl", [128, 1024])
    rfh = SBa("rfh", [128, 1024])
    rfl = SBa("rfl", [128, 1024])
    z1 = SBa("z1", [128, 256], F32)
    z2 = SBa("z2", [128, 1], F32)
    tt_trash = SBa("tt_trash", [128, 256], F32)
    barrier()
    for c in range(8):
        SY.wait_ge(VS.h, VS.n)
        LD.inc(SY.dma_start(out=off32[:], in_=cx.offwT[128 * c:128 * (c + 1), :]), 16)
        SC.wait_ge(LD.h, LD.n)
        SS.inc(SC.copy(out=offh[:, 256 * c:256 * (c + 1)], in_=off32[:]))
        V.wait_ge(SS.h, SS.n)
        VS.inc(V.tensor_tensor(out=offl[:, 256 * c:256 * (c + 1)], in0=off32[:],
                               in1=offh[:, 256 * c:256 * (c + 1)], op=ALU.subtract))
    for osp in range(4):
        bm = [VS.n, VS.n]
        for j in range(4):
            t = osp * 4 + j
            xs = xstream[:, 1024 * t:1024 * (t + 1)]
            _split_transpose_tile(cx, xs, xfh, xfl, lambda c, j=j: 512 * c + 128 * j,
                                  xh_t, xl_t, bm)
            SC.wait_ge(VS.h, VS.n)
            SS.inc(SC.activation(out=rh[:], in_=xs, func=AF.Relu))
            V.wait_ge(SS.h, SS.n)
            VS.inc(V.scalar_tensor_tensor(out=rl[:], in0=xs, scalar=0.0, in1=rh[:],
                                          op0=ALU.max, op1=ALU.subtract))
            T.wait_ge(SS.h, SS.n)
            T.wait_ge(VS.h, VS.n)
            for c in range(8):
                for hl, (src, dst) in enumerate(((rh, rfh), (rl, rfl))):
                    i = c * 2 + hl
                    pt = psE16 if i % 2 == 0 else psF16
                    T.wait_ge(VS.h, bm[i % 2])
                    TS.inc(T.transpose(out=pt[0:128, 0:128], in_=src[:, 128 * c:128 * (c + 1)],
                                       identity=cx.cid16[:]))
                    V.wait_ge(TS.h, TS.n)
                    VS.inc(V.tensor_copy(out=dst[:, 128 * c:128 * (c + 1)], in_=pt[0:128, 0:128]))
                    bm[i % 2] = VS.n
            T.wait_ge(VS.h, VS.n)
            for c in range(8):
                T.matmul(psA[:, 0:256], rfh[:, 128 * c:128 * (c + 1)],
                         offh[:, 256 * c:256 * (c + 1)], start=(c == 0), stop=False)
                T.matmul(psA[:, 0:256], rfh[:, 128 * c:128 * (c + 1)],
                         offl[:, 256 * c:256 * (c + 1)], start=False, stop=False)
                TS.inc(T.matmul(psA[:, 0:256], rfl[:, 128 * c:128 * (c + 1)],
                                offh[:, 256 * c:256 * (c + 1)], start=False, stop=(c == 7)))
            SC.wait_ge(TS.h, TS.n)
            SS.inc(SC.copy(out=z1[:], in_=psA[:, 0:256]))
            V.wait_ge(SS.h, SS.n)
            VS.inc(V.tensor_tensor(out=tt_trash[:], in0=z1[:], in1=cx.clin2bc[:], op=ALU.mult))
            VS.inc(V.reduce_sum(out=z2[:], in_=tt_trash[:], axis=AX.X))
            SC.wait_ge(VS.h, VS.n)
            SS.inc(SC.activation(out=cx.dxcol[:, t:t + 1], in_=z2[:], func=AF.Sigmoid,
                                 bias=cx.clin2bcol[:, 0:1], scale=1.0))
        G.wait_ge(VS.h, VS.n)
        for c in range(8):
            ST.inc(G.dma_start(out=cx.xfmo_h_d[128 * c:128 * (c + 1), 512 * osp:512 * (osp + 1)],
                               in_=xfh[:, 512 * c:512 * (c + 1)]), 16)
            ST.inc(G.dma_start(out=cx.xfmo_l_d[128 * c:128 * (c + 1), 512 * osp:512 * (osp + 1)],
                               in_=xfl[:, 512 * c:512 * (c + 1)]), 16)
        SC.wait_ge(ST.h, ST.n)
        V.wait_ge(ST.h, ST.n)
        T.wait_ge(ST.h, ST.n)
    V.wait_ge(SS.h, SS.n)
    VS.inc(V.tensor_scalar(out=cx.dxcol[:], in0=cx.dxcol[:], scalar1=float(S), scalar2=None,
                           op0=ALU.mult))
    cx.tap("dbg_dx", cx.dxcol[:])
    bm = [VS.n, VS.n]
    for t in range(NT):
        pt = psA if t % 2 == 0 else psB
        T.wait_ge(VS.h, bm[t % 2])
        TS.inc(T.transpose(out=pt[0:1, 0:128], in_=cx.dxcol[:, t:t + 1], identity=cx.cid32[:]))
        V.wait_ge(TS.h, TS.n)
        VS.inc(V.tensor_copy(out=cx.dxrow[0:1, 128 * t:128 * (t + 1)], in_=pt[0:1, 0:128]))
        bm[t % 2] = VS.n

    cx.tap("dbg_dxrow", cx.dxrow[:])
    esA.close()
    esB = ExitStack()
    SBb = lambda n, sh, dt=F16: esB.enter_context(nc.sbuf_tensor(n, sh, dt)).ap()
    kbh = SBb("kbh", [128, 2 * 4096])
    kbl = SBb("kbl", [128, 2 * 4096])
    vbh = SBb("vbh", [128, 32 * 256])
    vbl = SBb("vbl", [128, 32 * 256])
    qbh = SBb("qbh", [128, 2 * 2048])
    qbl = SBb("qbl", [128, 2 * 2048])
    Ebuf = SBb("Ebuf", [128, 2 * 512], F32)
    Phb = SBb("Phb", [128, 2 * 512])
    Plb = SBb("Plb", [128, 2 * 512])
    Rb = SBb("Rb", [128, 512])
    dxbc = SBb("dxbc", [128, 512], F32)
    o32 = SBb("o32", [128, 512], F32)
    zrow = SBb("zrow", [1, 512], F32)
    rz = SBb("rz", [128, 4], F32)

    for g in range(GROUPS):
        barrier()
        esW = ExitStack()
        SBw = lambda n, sh, dt=F16: esW.enter_context(nc.sbuf_tensor(f"{n}_g{g}", sh, dt)).ap()
        wgh = SBw("wgh", [128, 8 * 512])
        wgl = SBw("wgl", [128, 8 * 512])
        wg32 = SBw("wg32", [128, 256], F32)
        sfh = SBw("sfh", [128, 2 * 512])
        sfl = SBw("sfl", [128, 2 * 512])
        cvbbc = SBw("cvbbc", [128, 256], F32)
        LD.inc(SY.dma_start(out=cvbbc[:], in_=cx.vbbc_i[:, 256 * g:256 * (g + 1)]), 16)
        for c in range(8):
            for part, src_c0 in ((0, 1024 + g * 256), (1, 2048 + g * 256)):
                SY.wait_ge(VS.h, VS.n)
                LD.inc(SY.dma_start(out=wg32[:],
                                    in_=cx.qkvwT[128 * c:128 * (c + 1), src_c0:src_c0 + 256]), 16)
                wc = 512 * c + 256 * part
                SC.wait_ge(LD.h, LD.n)
                SS.inc(SC.copy(out=wgh[:, wc:wc + 256], in_=wg32[:]))
                V.wait_ge(SS.h, SS.n)
                VS.inc(V.tensor_tensor(out=wgl[:, wc:wc + 256], in0=wg32[:],
                                       in1=wgh[:, wc:wc + 256], op=ALU.subtract))
        def load_chunk(dram_h, dram_l, c, sp, buf):
            LD.inc(SY.dma_start(out=sfh[:, 512 * buf:512 * (buf + 1)],
                                in_=dram_h[128 * c:128 * (c + 1), 512 * sp:512 * (sp + 1)]), 16)
            LD.inc(SY.dma_start(out=sfl[:, 512 * buf:512 * (buf + 1)],
                                in_=dram_l[128 * c:128 * (c + 1), 512 * sp:512 * (sp + 1)]), 16)
            return LD.n
        for sp in range(8):
            marks = {}
            load_chunk(cx.xfm_h_d, cx.xfm_l_d, 0, sp, 0)
            marks[0] = LD.n
            T.wait_ge(VS.h, VS.n)
            T.wait_ge(SS.h, SS.n)
            if g == 0 and sp == 0:
                G.wait_ge(LD.h, LD.n)
                cx.tap("dbg_xf", sfh[:, 0:512])
            for c in range(8):
                buf = c % 2
                if c < 7:
                    load_chunk(cx.xfm_h_d, cx.xfm_l_d, c + 1, sp, (c + 1) % 2)
                    marks[c + 1] = LD.n
                T.wait_ge(LD.h, marks[c])
                sh = sfh[:, 512 * buf:512 * (buf + 1)]
                sl = sfl[:, 512 * buf:512 * (buf + 1)]
                for oc in range(2):
                    ps = psA if oc == 0 else psB
                    wc = 512 * c + 128 * oc
                    T.matmul(ps[:], wgh[:, wc:wc + 128], sh, start=(c == 0), stop=False)
                    T.matmul(ps[:], wgh[:, wc:wc + 128], sl, start=False, stop=False)
                    TS.inc(T.matmul(ps[:], wgl[:, wc:wc + 128], sh, start=False, stop=(c == 7)))
                if c < 7:
                    pass
                SY.wait_ge(TS.h, TS.n)
            for oc in range(2):
                ps = psA if oc == 0 else psB
                bias = cx.cqkvbT[:, 8 + 2 * g + oc:8 + 2 * g + oc + 1]
                kc = 4096 * oc + 512 * sp
                SC.wait_ge(TS.h, TS.n)
                SS.inc(SC.activation(out=kbh[:, kc:kc + 512], in_=ps[:], func=AF.Identity,
                                     bias=bias, scale=1.0))
                V.wait_ge(SS.h, SS.n)
                VS.inc(V.scalar_tensor_tensor(out=kbl[:, kc:kc + 512], in0=ps[:], scalar=bias,
                                              in1=kbh[:, kc:kc + 512],
                                              op0=ALU.add, op1=ALU.subtract))
            if g == 0 and sp == 0:
                cx.tap("dbg_Ph2", kbh[:, 0:512])
            marks = {}
            load_chunk(cx.xfm_h_d, cx.xfm_l_d, 0, sp, 0)
            marks[0] = LD.n
            T.wait_ge(VS.h, VS.n)
            T.wait_ge(SS.h, SS.n)
            for c in range(8):
                buf = c % 2
                if c < 7:
                    load_chunk(cx.xfm_h_d, cx.xfm_l_d, c + 1, sp, (c + 1) % 2)
                    marks[c + 1] = LD.n
                T.wait_ge(LD.h, marks[c])
                wc = 512 * c + 256
                vbanks = [psA[:, 0:256], psB[:, 0:256], psO[:, 0:256], psO[:, 512:768]]
                for j in range(4):
                    sh = sfh[:, 512 * buf + 128 * j:512 * buf + 128 * (j + 1)]
                    sl = sfl[:, 512 * buf + 128 * j:512 * buf + 128 * (j + 1)]
                    po = vbanks[j]
                    T.matmul(po, sh, wgh[:, wc:wc + 256], start=(c == 0), stop=False)
                    T.matmul(po, sh, wgl[:, wc:wc + 256], start=False, stop=False)
                    TS.inc(T.matmul(po, sl, wgh[:, wc:wc + 256], start=False, stop=(c == 7)))
                SY.wait_ge(TS.h, TS.n)
            for j in range(4):
                gt = sp * 4 + j
                po = [psA[:, 0:256], psB[:, 0:256], psO[:, 0:256], psO[:, 512:768]][j]
                V.wait_ge(TS.h, TS.n)
                VS.inc(V.tensor_tensor(out=po, in0=po, in1=cvbbc[:], op=ALU.add))
                SC.wait_ge(VS.h, VS.n)
                SS.inc(SC.copy(out=vbh[:, 256 * gt:256 * (gt + 1)], in_=po))
                V.wait_ge(SS.h, SS.n)
                VS.inc(V.tensor_tensor(out=vbl[:, 256 * gt:256 * (gt + 1)], in0=po,
                                       in1=vbh[:, 256 * gt:256 * (gt + 1)], op=ALU.subtract))
        barrier()
        for c in range(8):
            SY.wait_ge(VS.h, VS.n)
            LD.inc(SY.dma_start(out=wg32[:],
                                in_=cx.qkvwT[128 * c:128 * (c + 1), g * 256:g * 256 + 256]), 16)
            wc = 512 * c
            SC.wait_ge(LD.h, LD.n)
            SS.inc(SC.copy(out=wgh[:, wc:wc + 256], in_=wg32[:]))
            V.wait_ge(SS.h, SS.n)
            VS.inc(V.tensor_tensor(out=wgl[:, wc:wc + 256], in0=wg32[:],
                                   in1=wgh[:, wc:wc + 256], op=ALU.subtract))
        for osp in range(4):
            marks = {}
            load_chunk(cx.xfmo_h_d, cx.xfmo_l_d, 0, osp, 0)
            marks[0] = LD.n
            T.wait_ge(VS.h, VS.n)
            T.wait_ge(SS.h, SS.n)
            for c in range(8):
                buf = c % 2
                if c < 7:
                    load_chunk(cx.xfmo_h_d, cx.xfmo_l_d, c + 1, osp, (c + 1) % 2)
                    marks[c + 1] = LD.n
                T.wait_ge(LD.h, marks[c])
                sh = sfh[:, 512 * buf:512 * (buf + 1)]
                sl = sfl[:, 512 * buf:512 * (buf + 1)]
                for oc in range(2):
                    ps = psA if oc == 0 else psB
                    wc = 512 * c + 128 * oc
                    T.matmul(ps[:], wgh[:, wc:wc + 128], sh, start=(c == 0), stop=False)
                    T.matmul(ps[:], wgh[:, wc:wc + 128], sl, start=False, stop=False)
                    TS.inc(T.matmul(ps[:], wgl[:, wc:wc + 128], sh, start=False, stop=(c == 7)))
                SY.wait_ge(TS.h, TS.n)
            for oc in range(2):
                ps = psA if oc == 0 else psB
                bias = cx.cqkvbT[:, 2 * g + oc:2 * g + oc + 1]
                qc = 2048 * oc + 512 * osp
                SC.wait_ge(TS.h, TS.n)
                SS.inc(SC.activation(out=qbh[:, qc:qc + 512], in_=ps[:], func=AF.Identity,
                                     bias=bias, scale=1.0))
                V.wait_ge(SS.h, SS.n)
                VS.inc(V.scalar_tensor_tensor(out=qbl[:, qc:qc + 512], in0=ps[:], scalar=bias,
                                              in1=qbh[:, qc:qc + 512],
                                              op0=ALU.add, op1=ALU.subtract))

        if g == 0:
            cx.tap("dbg_kb", kbh[:, 0:1024])
            cx.tap("dbg_qb", qbh[:, 0:1024])
            cx.tap("dbg_vb", vbh[:, 0:512])
        barrier()
        esW.close()
        emark = [SS.n, SS.n]
        pmark = [TS.n, TS.n]
        for qi in range(4):
            q0 = 512 * qi
            T.wait_ge(VS.h, VS.n)
            TS.inc(T.matmul(psB[:], cx.cones1x128[:], cx.dxrow[0:1, q0:q0 + 512],
                            start=True, stop=True))
            V.wait_ge(TS.h, TS.n)
            VS.inc(V.tensor_tensor(out=dxbc[:], in0=psB[:], in1=cx.cqpos[:, q0:q0 + 512],
                                   op=ALU.add))
            T.wait_ge(VS.h, VS.n)
            nk = 32 - KST0[qi]
            first_pv = True
            for ki in range(nk):
                ktc = KST0[qi] + ki
                bb = ki % 2
                scps = psA if bb == 0 else psB
                T.wait_ge(SS.h, emark[bb])
                for oc in range(2):
                    kc0 = 4096 * oc + 128 * ktc
                    qc0 = 2048 * oc + q0
                    T.matmul(scps[:], kbh[:, kc0:kc0 + 128], qbh[:, qc0:qc0 + 512],
                             start=(oc == 0), stop=False)
                    T.matmul(scps[:], kbh[:, kc0:kc0 + 128], qbl[:, qc0:qc0 + 512],
                             start=False, stop=False)
                    TS.inc(T.matmul(scps[:], kbl[:, kc0:kc0 + 128], qbh[:, qc0:qc0 + 512],
                                    start=False, stop=(oc == 1)))
                SC.wait_ge(TS.h, TS.n)
                SC.wait_ge(VS.h, VS.n)
                SS.inc(SC.activation(out=Ebuf[:, 512 * bb:512 * (bb + 1)], in_=scps[:],
                                     func=AF.Exp, scale=1.0 / 32.0))
                emark[bb] = SS.n
                V.wait_ge(SS.h, SS.n)
                V.wait_ge(TS.h, pmark[bb])
                bkcol = cx.cbkt[:, 32 * g + ktc:32 * g + ktc + 1]
                eb = Ebuf[:, 512 * bb:512 * (bb + 1)]
                phb = Phb[:, 512 * bb:512 * (bb + 1)]
                plb = Plb[:, 512 * bb:512 * (bb + 1)]
                rb = Rb[:, 0:512]
                VS.inc(V.scalar_tensor_tensor(out=phb, in0=dxbc[:], scalar=bkcol, in1=eb,
                                              op0=ALU.is_ge, op1=ALU.mult))
                VS.inc(V.tensor_tensor(out=rb, in0=eb, in1=phb, op=ALU.subtract))
                VS.inc(V.scalar_tensor_tensor(out=plb, in0=dxbc[:], scalar=bkcol, in1=rb,
                                              op0=ALU.is_ge, op1=ALU.mult))
                if g == 0 and qi == 0 and ki == 2:
                    cx.tap("dbg_E", Ebuf[:, 512 * bb:512 * (bb + 1)])
                    cx.tap("dbg_Ph", Phb[:, 512 * bb:512 * (bb + 1)])
                    cx.tap("dbg_dxbc", dxbc[:])
                if ki < L0_CAUSAL_W:
                    ktcol = cx.cktpos[:, ktc:ktc + 1]
                    VS.inc(V.tensor_scalar(out=rb, in0=cx.cqpos[:, q0:q0 + 512],
                                           scalar1=ktcol, scalar2=None, op0=ALU.is_le))
                    VS.inc(V.tensor_tensor(out=phb, in0=phb, in1=rb, op=ALU.mult))
                    VS.inc(V.tensor_tensor(out=plb, in0=plb, in1=rb, op=ALU.mult))
                T.wait_ge(VS.h, VS.n)
                T.matmul(psR[:], cx.cones16[:], phb, start=first_pv, stop=False)
                TS.inc(T.matmul(psR[:], cx.cones16[:], plb, start=False, stop=(ki == nk - 1)))
                for fc in range(2):
                    vc0 = 256 * ktc + 128 * fc
                    po = psO[:, 512 * fc:512 * (fc + 1)]
                    T.matmul(po, vbh[:, vc0:vc0 + 128], phb, start=first_pv, stop=False)
                    T.matmul(po, vbh[:, vc0:vc0 + 128], plb, start=False, stop=False)
                    TS.inc(T.matmul(po, vbl[:, vc0:vc0 + 128], phb, start=False,
                                    stop=(ki == nk - 1)))
                pmark[bb] = TS.n
                first_pv = False
            SC.wait_ge(TS.h, TS.n)
            SS.inc(SC.copy(out=zrow[:], in_=psR[:]))
            if g == 0 and qi == 0:
                cx.tap("dbg_zrow", zrow[:])
            T.wait_ge(SS.h, SS.n)
            for j in range(4):
                TS.inc(T.matmul(psB[0:128, j:j + 1], zrow[0:1, 128 * j:128 * (j + 1)],
                                cx.cone1[:], start=(j == 0), stop=(j == 3)))
            V.wait_ge(TS.h, TS.n)
            VS.inc(V.reciprocal(out=rz[:], in_=psB[0:128, 0:4]))
            if g == 0 and qi == 0:
                cx.tap("dbg_rz", rz[:])
            evm = VS.n
            for fc in range(2):
                SC.wait_ge(VS.h, evm)
                SS.inc(SC.copy(out=o32[:], in_=psO[:, 512 * fc:512 * (fc + 1)]))
                if g == 0 and qi == 0 and fc == 0:
                    cx.tap("dbg_o32", o32[:])
                T.wait_ge(SS.h, SS.n)
                for j in range(4):
                    t = qi * 4 + j
                    T.wait_ge(VS.h, evm)
                    TS.inc(T.transpose(out=psA[0:128, 0:128],
                                       in_=o32[:, 128 * j:128 * (j + 1)],
                                       identity=cx.cid32[:]))
                    V.wait_ge(TS.h, TS.n)
                    col = 1024 * t + 256 * g + 128 * fc
                    VS.inc(V.scalar_tensor_tensor(
                        out=xstream[:, col:col + 128], in0=psA[0:128, 0:128],
                        scalar=rz[:, j:j + 1], in1=xstream[:, col:col + 128],
                        op0=ALU.mult, op1=ALU.add,
                        accum_out=cx.accln[:, 8 * t + 2 * g + fc:8 * t + 2 * g + fc + 1]))
                    evm = VS.n

    cx.tap("dbg_vst", xstream[:, 0:1024])
    barrier()
    esB.close()
    esC = ExitStack()
    _ln_round(cx, esC)
    barrier()
    esC.close()


def _ln_round(cx, phase_es, tag="a"):
    nc = cx.nc
    T, V, SC = cx.T, cx.V, cx.SC
    TS, SS, VS = cx.TS, cx.SS, cx.VS
    xstream, accln, sqacc = cx.xstream, cx.accln, cx.sqacc
    sq16 = phase_es.enter_context(nc.sbuf_tensor("sq16_" + tag, [128, 1024], F16)).ap()
    mv = phase_es.enter_context(nc.sbuf_tensor("mvbuf_" + tag, [128, 6 * NT], F32)).ap()
    sv = mv[:, 0 * NT:1 * NT]
    nm = mv[:, 1 * NT:2 * NT]
    aa = mv[:, 2 * NT:3 * NT]
    bb2 = mv[:, 3 * NT:4 * NT]
    rv = mv[:, 4 * NT:5 * NT]
    rr = mv[:, 5 * NT:6 * NT]
    for t in range(NT):
        SS.inc(SC.activation(out=sq16[:], in_=xstream[:, 1024 * t:1024 * (t + 1)],
                             func=AF.Square, accum_out=sqacc[:, t:t + 1]))
    V.wait_ge(SS.h, SS.n)
    for t in range(NT):
        VS.inc(V.reduce_sum(out=sv[:, t:t + 1], in_=accln[:, 8 * t:8 * (t + 1)], axis=AX.X))
    V.wait_ge(VS.h, VS.n)
    VS.inc(V.tensor_scalar(out=nm[:], in0=sv[:], scalar1=-1.0 / 1024.0, scalar2=0.0,
                           op0=ALU.mult, op1=ALU.add))
    VS.inc(V.tensor_scalar(out=aa[:], in0=sqacc[:, 0:NT], scalar1=1.0 / 1024.0, scalar2=1e-5,
                           op0=ALU.mult, op1=ALU.add))
    V.wait_ge(VS.h, VS.n)
    VS.inc(V.tensor_tensor(out=bb2[:], in0=nm[:], in1=nm[:], op=ALU.mult))
    V.wait_ge(VS.h, VS.n)
    VS.inc(V.tensor_tensor(out=rv[:], in0=aa[:], in1=bb2[:], op=ALU.subtract))
    V.wait_ge(VS.h, VS.n)
    VS.inc(V.reciprocal(out=rv[:], in_=rv[:]))
    cx.tap("dbg_acc", accln[:])
    cx.tap("dbg_sq", sqacc[:])
    SC.wait_ge(VS.h, VS.n)
    SS.inc(SC.activation(out=rr[:], in_=rv[:], func=AF.Sqrt))
    V.wait_ge(SS.h, SS.n)
    cx.tap("dbg_mv", mv[:, 0:96])
    for t in range(NT):
        VS.inc(V.tensor_scalar(out=xstream[:, 1024 * t:1024 * (t + 1)],
                               in0=xstream[:, 1024 * t:1024 * (t + 1)],
                               scalar1=nm[:, t:t + 1],
                               scalar2=rr[:, t:t + 1],
                               op0=ALU.add, op1=ALU.mult))


def _l0ffn(cx):
    nc = cx.nc
    T, V, SC, SY, G = cx.T, cx.V, cx.SC, cx.SY, cx.G
    LD, ST, TS, SS, VS = cx.LD, cx.ST, cx.TS, cx.SS, cx.VS
    psA, psB, psO = cx.psA, cx.psB, cx.psO
    psE16, psF16 = cx.psE16, cx.psF16
    xstream = cx.xstream
    es = ExitStack()
    SB = lambda n, sh, dt=F16: es.enter_context(nc.sbuf_tensor(n, sh, dt)).ap()
    stage32 = SB("f0_stage", [128, 512], F32)
    cwdh = SB("f0_wdh", [128, 8 * 512])
    cwdl = SB("f0_wdl", [128, 8 * 512])
    cwgh = SB("f0_wgh", [128, 8 * 8])
    cwgl = SB("f0_wgl", [128, 8 * 8])
    cwuh = SB("f0_wuh", [128, 4 * 1024])
    cwul = SB("f0_wul", [128, 4 * 1024])
    cbdbc = SB("f0_bdbc", [128, 512], F32)
    cbu32 = SB("f0_bu", [8, 1024], F32)
    x1h = SB("f0_x1h", [128, 1024])
    x1l = SB("f0_x1l", [128, 1024])
    xfmh = SB("f0_xfmh", [128, 1024])
    xfml = SB("f0_xfml", [128, 1024])
    h32 = SB("f0_h32", [128, 512], F32)
    lg = SB("f0_lg", [128, 8], F32)
    gt8 = SB("f0_gt", [128, 16], F32)
    gsc = SB("f0_gsc", [128, 8], F32)
    gT32 = SB("f0_gT32", [8, 128], F32)
    sh_t = SB("f0_sh", [128, 512])
    sl_t = SB("f0_sl", [128, 512])
    sfmh = SB("f0_sfmh", [128, 512])
    sfml = SB("f0_sfml", [128, 512])

    cx.barrier()
    VS.inc(V.memset(cx.accln[:], 0.0))
    LD.inc(SY.dma_start(out=cbdbc[:], in_=cx.bdbc_i[:]), 16)
    LD.inc(SY.dma_start(out=cbu32[:], in_=cx.bu8[:]), 16)
    for c in range(8):
        SY.wait_ge(VS.h, VS.n)
        LD.inc(SY.dma_start(out=stage32[:], in_=cx.wdflat[128 * c:128 * (c + 1), :]), 16)
        SC.wait_ge(LD.h, LD.n)
        SS.inc(SC.copy(out=cwdh[:, 512 * c:512 * (c + 1)], in_=stage32[:]))
        V.wait_ge(SS.h, SS.n)
        VS.inc(V.tensor_tensor(out=cwdl[:, 512 * c:512 * (c + 1)], in0=stage32[:],
                               in1=cwdh[:, 512 * c:512 * (c + 1)], op=ALU.subtract))
        SY.wait_ge(VS.h, VS.n)
        LD.inc(SY.dma_start(out=stage32[:, 0:8], in_=cx.wgate8[128 * c:128 * (c + 1), :]), 16)
        SC.wait_ge(LD.h, LD.n)
        SS.inc(SC.copy(out=cwgh[:, 8 * c:8 * (c + 1)], in_=stage32[:, 0:8]))
        V.wait_ge(SS.h, SS.n)
        VS.inc(V.tensor_tensor(out=cwgl[:, 8 * c:8 * (c + 1)], in0=stage32[:, 0:8],
                               in1=cwgh[:, 8 * c:8 * (c + 1)], op=ALU.subtract))
    for c in range(4):
        SY.wait_ge(VS.h, VS.n)
        LD.inc(SY.dma_start(out=stage32[:], in_=cx.wustack[128 * c:128 * (c + 1), 0:512]), 16)
        SC.wait_ge(LD.h, LD.n)
        SS.inc(SC.copy(out=cwuh[:, 1024 * c:1024 * c + 512], in_=stage32[:]))
        V.wait_ge(SS.h, SS.n)
        VS.inc(V.tensor_tensor(out=cwul[:, 1024 * c:1024 * c + 512], in0=stage32[:],
                               in1=cwuh[:, 1024 * c:1024 * c + 512], op=ALU.subtract))
        SY.wait_ge(VS.h, VS.n)
        LD.inc(SY.dma_start(out=stage32[:], in_=cx.wustack[128 * c:128 * (c + 1), 512:1024]), 16)
        SC.wait_ge(LD.h, LD.n)
        SS.inc(SC.copy(out=cwuh[:, 1024 * c + 512:1024 * (c + 1)], in_=stage32[:]))
        V.wait_ge(SS.h, SS.n)
        VS.inc(V.tensor_tensor(out=cwul[:, 1024 * c + 512:1024 * (c + 1)], in0=stage32[:],
                               in1=cwuh[:, 1024 * c + 512:1024 * (c + 1)], op=ALU.subtract))

    for t in range(NT):
        xs = xstream[:, 1024 * t:1024 * (t + 1)]
        bm = [VS.n, VS.n]
        _split_transpose_tile(cx, xs, xfmh, xfml, lambda c: 128 * c, x1h, x1l, bm)
        T.wait_ge(VS.h, VS.n)
        for c in range(8):
            xh = xfmh[:, 128 * c:128 * (c + 1)]
            xl = xfml[:, 128 * c:128 * (c + 1)]
            T.matmul(psO[:, 0:512], xh, cwdh[:, 512 * c:512 * (c + 1)], start=(c == 0), stop=False)
            T.matmul(psO[:, 0:512], xh, cwdl[:, 512 * c:512 * (c + 1)], start=False, stop=False)
            TS.inc(T.matmul(psO[:, 0:512], xl, cwdh[:, 512 * c:512 * (c + 1)], start=False,
                            stop=(c == 7)))
            T.matmul(psB[:, 0:8], xh, cwgh[:, 8 * c:8 * (c + 1)], start=(c == 0), stop=False)
            T.matmul(psB[:, 0:8], xh, cwgl[:, 8 * c:8 * (c + 1)], start=False, stop=False)
            TS.inc(T.matmul(psB[:, 0:8], xl, cwgh[:, 8 * c:8 * (c + 1)], start=False,
                            stop=(c == 7)))
        SC.wait_ge(TS.h, TS.n)
        SS.inc(SC.copy(out=lg[:], in_=psB[:, 0:8]))
        V.wait_ge(SS.h, SS.n)
        V.wait_ge(TS.h, TS.n)
        VS.inc(V.tensor_tensor(out=h32[:], in0=psO[:, 0:512], in1=cbdbc[:], op=ALU.add))
        VS.inc(V.reduce_max(out=gt8[:, 0:1], in_=lg[:], axis=AX.X))
        V.wait_ge(VS.h, VS.n)
        VS.inc(V.tensor_scalar(out=gt8[:, 1:2], in0=gt8[:, 0:1], scalar1=-1.0, scalar2=0.0,
                               op0=ALU.mult, op1=ALU.add))
        V.wait_ge(VS.h, VS.n)
        VS.inc(V.tensor_scalar(out=gt8[:, 2:10], in0=lg[:], scalar1=gt8[:, 1:2], scalar2=0.0,
                               op0=ALU.add, op1=ALU.is_equal))
        V.wait_ge(VS.h, VS.n)
        VS.inc(V.scalar_tensor_tensor(out=gt8[:, 2:10], in0=gt8[:, 2:10], scalar=-1e30,
                                      in1=lg[:], op0=ALU.mult, op1=ALU.add))
        V.wait_ge(VS.h, VS.n)
        VS.inc(V.reduce_max(out=gt8[:, 10:11], in_=gt8[:, 2:10], axis=AX.X))
        V.wait_ge(VS.h, VS.n)
        VS.inc(V.tensor_scalar(out=gt8[:, 11:12], in0=gt8[:, 10:11], scalar1=-1.0, scalar2=0.0,
                               op0=ALU.mult, op1=ALU.add))
        SC.wait_ge(VS.h, VS.n)
        SS.inc(SC.activation(out=gt8[:, 2:10], in_=lg[:], func=AF.Exp, bias=gt8[:, 1:2],
                             scale=1.0))
        V.wait_ge(SS.h, SS.n)
        VS.inc(V.tensor_scalar(out=gsc[:], in0=lg[:], scalar1=gt8[:, 11:12], scalar2=0.0,
                               op0=ALU.add, op1=ALU.is_ge))
        V.wait_ge(VS.h, VS.n)
        VS.inc(V.scalar_tensor_tensor(out=gsc[:], in0=gt8[:, 2:10], scalar=1.0, in1=gsc[:],
                                      op0=ALU.mult, op1=ALU.mult,
                                      accum_out=gt8[:, 12:13]))
        V.wait_ge(VS.h, VS.n)
        VS.inc(V.reciprocal(out=gt8[:, 13:14], in_=gt8[:, 12:13]))
        V.wait_ge(VS.h, VS.n)
        VS.inc(V.tensor_scalar(out=gsc[:], in0=gsc[:], scalar1=gt8[:, 13:14], scalar2=0.5,
                               op0=ALU.mult, op1=ALU.mult))
        SC.wait_ge(VS.h, VS.n)
        SS.inc(SC.activation(out=h32[:], in_=h32[:], func=AF.Relu))
        V.wait_ge(SS.h, SS.n)
        for e in range(8):
            VS.inc(V.tensor_scalar(out=sh_t[:, 64 * e:64 * (e + 1)], in0=h32[:, 64 * e:64 * (e + 1)],
                                   scalar1=gsc[:, e:e + 1], scalar2=0.0, op0=ALU.mult, op1=ALU.add))
        V.wait_ge(VS.h, VS.n)
        for e in range(8):
            VS.inc(V.scalar_tensor_tensor(out=sl_t[:, 64 * e:64 * (e + 1)],
                                          in0=h32[:, 64 * e:64 * (e + 1)], scalar=gsc[:, e:e + 1],
                                          in1=sh_t[:, 64 * e:64 * (e + 1)],
                                          op0=ALU.mult, op1=ALU.subtract))
        T.wait_ge(VS.h, VS.n)
        TS.inc(T.transpose(out=psA[0:8, 0:128], in_=gsc[:], identity=cx.cid32[:]))
        V.wait_ge(TS.h, TS.n)
        VS.inc(V.tensor_copy(out=gT32[:], in_=psA[0:8, 0:128]))
        bm2 = [VS.n, VS.n]
        for c4 in range(4):
            for hl, (srcb, dstb) in enumerate(((sh_t, sfmh), (sl_t, sfml))):
                i = c4 * 2 + hl
                pt = psE16 if i % 2 == 0 else psF16
                T.wait_ge(VS.h, bm2[i % 2])
                TS.inc(T.transpose(out=pt[0:128, 0:128], in_=srcb[:, 128 * c4:128 * (c4 + 1)],
                                   identity=cx.cid16[:]))
                V.wait_ge(TS.h, TS.n)
                VS.inc(V.tensor_copy(out=dstb[:, 128 * c4:128 * (c4 + 1)], in_=pt[0:128, 0:128]))
                bm2[i % 2] = VS.n
        T.wait_ge(VS.h, VS.n)
        for half in range(2):
            po = psO[:, 512 * half:512 * (half + 1)]
            for c4 in range(4):
                sh = sfmh[:, 128 * c4:128 * (c4 + 1)]
                sl = sfml[:, 128 * c4:128 * (c4 + 1)]
                wy = cwuh[:, 1024 * c4 + 512 * half:1024 * c4 + 512 * (half + 1)]
                wyl = cwul[:, 1024 * c4 + 512 * half:1024 * c4 + 512 * (half + 1)]
                T.matmul(po, sh, wy, start=(c4 == 0), stop=False)
                T.matmul(po, sh, wyl, start=False, stop=False)
                TS.inc(T.matmul(po, sl, wy, start=False, stop=False))
            TS.inc(T.matmul(po, gT32[:], cbu32[:, 512 * half:512 * (half + 1)],
                            start=False, stop=True))
        V.wait_ge(TS.h, TS.n)
        for half in range(2):
            VS.inc(V.scalar_tensor_tensor(out=xs[:, 512 * half:512 * (half + 1)],
                                          in0=psO[:, 512 * half:512 * (half + 1)], scalar=1.0,
                                          in1=xs[:, 512 * half:512 * (half + 1)],
                                          op0=ALU.mult, op1=ALU.add,
                                          accum_out=cx.accln[:, 8 * t + half:8 * t + half + 1]))
    cx.barrier()
    esl = ExitStack()
    _ln_round(cx, esl, "f0")
    cx.barrier()
    esl.close()
    es.close()


def _l1attn(cx, do_ln=True):
    nc = cx.nc
    T, V, SC, SY, G = cx.T, cx.V, cx.SC, cx.SY, cx.G
    LD, ST, TS, SS, VS = cx.LD, cx.ST, cx.TS, cx.SS, cx.VS
    psA, psB, psO, psR = cx.psA, cx.psB, cx.psO, cx.psR
    psE16, psF16 = cx.psE16, cx.psF16
    xstream = cx.xstream
    es = ExitStack()
    SB = lambda n, sh, dt=F16: es.enter_context(nc.sbuf_tensor(n, sh, dt)).ap()

    pid_s = SB("a1_pid", [1, 1], I32)
    fone = SB("a1_fone", [1, 1], I32)
    qpc_u = SB("a1_qpcu", [128, 16], U32)
    qpc_f = SB("a1_qpcf", [128, 16], F32)
    cx.barrier()
    LD.inc(SY.dma_start(out=pid_s[:], in_=cx.pid[:]), 16)
    LD.inc(SY.dma_start(out=qpc_f[:], in_=cx.qposcol[:]), 16)
    V.wait_ge(LD.h, LD.n)
    VS.inc(V.memset(fone[:], 1))
    VS.inc(V.tensor_copy(out=qpc_u[:], in_=qpc_f[:]))
    G.wait_ge(VS.h, VS.n)
    for t in range(NT):
        ST.inc(G.indirect_dma_start(
            out=cx.x2sh[:, :], out_offset=IndirectOffsetOnAxis(ap=qpc_u[:, t:t + 1], axis=0),
            in_=xstream[:, 1024 * t:1024 * (t + 1)], in_offset=None), 16)
    G.wait_ge(ST.h, ST.n)
    if cx.dbg:
        es0x = ExitStack()
        _t0 = es0x.enter_context(nc.sbuf_tensor("a1_dbg0", [128, 1024], F32)).ap()
        _tq = es0x.enter_context(nc.sbuf_tensor("a1_dbgq", [128, 16], F32)).ap()
        G.dma_start(out=_t0[:], in_=cx.x2sh[0:128, :]).then_inc(ST.h, 16)
        ST.n += 16
        G.wait_ge(ST.h, ST.n)
        cx.tap("d1_post_scatter", _t0[:])
        V.wait_ge(cx.ST.h, cx.ST.n)
        VS.inc(V.tensor_copy(out=_tq[:], in_=qpc_u[:]))
        cx.tap("d1_qpc", _tq[:])
        cx.tap("d1_xs0", xstream[:, 0:1024])
        es0x.close()
    with G.register("a1_pr") as pr:
        G.load(pr, pid_s[0:1, 0:1])
        with G.If_cmp(pr, 1, "IS_LT"):
            G.dma_start(out=cx.exflag[0:1, :], in_=fone[:]).then_inc(ST.h, 16)
        with G.Else():
            G.dma_start(out=cx.exflag[1:2, :], in_=fone[:]).then_inc(ST.h, 16)
        G.end_ifs()
        ST.n += 16
        G.wait_ge(ST.h, ST.n)
        for fi, bbn in ((0, "a1_poll0"), (1, "a1_poll1")):
            with G.register(f"a1_f{fi}") as fr:
                G.reg_mov(fr, 0)
                G.br(bbn)
                with nc.body(bbn, valid_engines=[G.engine]):
                    G.load(fr, cx.exflag[fi:fi + 1, 0:1])
                    G.br_lt(fr, 1, on_true=bbn, on_false=bbn + "_done")
                nc.switch_bb(bbn + "_done")
    ST.inc(G.dma_start(out=cx.exflag[0:1, :].__class__ and cx.x2sh[4095:4096, 0:1],
                       in_=fone[:]) if False else G.dma_start(out=fone[:], in_=cx.exflag[0:1, :]), 16)
    G.wait_ge(ST.h, ST.n)
    for e in (T, V, SC, SY):
        e.wait_ge(ST.h, ST.n)

    es1x = ExitStack()
    if cx.dbg:
        _t = es1x.enter_context(nc.sbuf_tensor("a1_dbgt", [128, 1024], F32)).ap()
        _t2 = es1x.enter_context(nc.sbuf_tensor("a1_dbgt2", [128, 1024], F32)).ap()
        LD.inc(SY.dma_start(out=_t[:], in_=cx.x2sh[0:128, :]), 16)
        G.wait_ge(LD.h, LD.n)
        cx.tap("d1_x2sh", _t[:])
        ST.inc(G.dma_start(out=_t2[:], in_=cx.x2sh[0:128, :]), 16)
        G.wait_ge(ST.h, ST.n)
        cx.tap("d1_x2sh_g", _t2[:])
    es1x.close()
    es1 = ExitStack()
    SB1 = lambda n, sh, dt=F16: es1.enter_context(nc.sbuf_tensor(n, sh, dt)).ap()
    xin32 = SB1("a1_xin32", [128, 1024], F32)
    xh_t = SB1("a1_xh", [128, 1024])
    xl_t = SB1("a1_xl", [128, 1024])
    xfh = SB1("a1_xfh", [128, 4096])
    xfl = SB1("a1_xfl", [128, 4096])
    for sp in range(8):
        bm = [VS.n, VS.n]
        for j in range(4):
            gt = sp * 4 + j
            SY.wait_ge(VS.h, VS.n)
            LD.inc(SY.dma_start(out=xin32[:], in_=cx.x2sh[128 * gt:128 * (gt + 1), :]), 16)
            SC.wait_ge(LD.h, LD.n)
            V.wait_ge(LD.h, LD.n)
            _split_transpose_tile(cx, xin32, xfh, xfl, lambda c, j=j: 512 * c + 128 * j,
                                  xh_t, xl_t, bm)
        G.wait_ge(VS.h, VS.n)
        for c in range(8):
            ST.inc(G.dma_start(out=cx.xfm_h_d[128 * c:128 * (c + 1), 512 * sp:512 * (sp + 1)],
                               in_=xfh[:, 512 * c:512 * (c + 1)]), 16)
            ST.inc(G.dma_start(out=cx.xfm_l_d[128 * c:128 * (c + 1), 512 * sp:512 * (sp + 1)],
                               in_=xfl[:, 512 * c:512 * (c + 1)]), 16)
        SC.wait_ge(ST.h, ST.n)
        V.wait_ge(ST.h, ST.n)
        T.wait_ge(ST.h, ST.n)
    for osp in range(4):
        bm = [VS.n, VS.n]
        for j in range(4):
            t = osp * 4 + j
            _split_transpose_tile(cx, xstream[:, 1024 * t:1024 * (t + 1)], xfh, xfl,
                                  lambda c, j=j: 512 * c + 128 * j, xh_t, xl_t, bm)
        G.wait_ge(VS.h, VS.n)
        for c in range(8):
            ST.inc(G.dma_start(out=cx.xfmo_h_d[128 * c:128 * (c + 1), 512 * osp:512 * (osp + 1)],
                               in_=xfh[:, 512 * c:512 * (c + 1)]), 16)
            ST.inc(G.dma_start(out=cx.xfmo_l_d[128 * c:128 * (c + 1), 512 * osp:512 * (osp + 1)],
                               in_=xfl[:, 512 * c:512 * (c + 1)]), 16)
        SC.wait_ge(ST.h, ST.n)
        V.wait_ge(ST.h, ST.n)
        T.wait_ge(ST.h, ST.n)

    cx.barrier()
    es1.close()
    cwbh = SB("a1_cwbh", [128, 8 * 1024])
    cwbl = SB("a1_cwbl", [128, 8 * 1024])
    stage32 = SB("a1_stage", [128, 512], F32)
    sfh = SB("a1_sfh", [128, 8 * 128])
    sfl = SB("a1_sfl", [128, 8 * 128])
    g32 = SB("a1_g32", [128, 1024], F32)
    cx.barrier()
    for c in range(8):
        for half in range(2):
            SY.wait_ge(VS.h, VS.n)
            LD.inc(SY.dma_start(out=stage32[:],
                                in_=cx.wg_l1[128 * c:128 * (c + 1), 512 * half:512 * (half + 1)]), 16)
            wc = 1024 * c + 512 * half
            SC.wait_ge(LD.h, LD.n)
            SS.inc(SC.copy(out=cwbh[:, wc:wc + 512], in_=stage32[:]))
            V.wait_ge(SS.h, SS.n)
            VS.inc(V.tensor_tensor(out=cwbl[:, wc:wc + 512], in0=stage32[:],
                                   in1=cwbh[:, wc:wc + 512], op=ALU.subtract))
    for t in range(NT):
        SY.wait_ge(TS.h, TS.n)
        for c in range(8):
            LD.inc(SY.dma_start(out=sfh[:, 128 * c:128 * (c + 1)],
                                in_=cx.xfmo_h_d[128 * c:128 * (c + 1), 128 * t:128 * (t + 1)]), 16)
            LD.inc(SY.dma_start(out=sfl[:, 128 * c:128 * (c + 1)],
                                in_=cx.xfmo_l_d[128 * c:128 * (c + 1), 128 * t:128 * (t + 1)]), 16)
        T.wait_ge(LD.h, LD.n)
        T.wait_ge(SS.h, SS.n)
        for half in range(2):
            po = psO[:, 512 * half:512 * (half + 1)]
            for c in range(8):
                xh = sfh[:, 128 * c:128 * (c + 1)]
                xl = sfl[:, 128 * c:128 * (c + 1)]
                wc = 1024 * c + 512 * half
                T.matmul(po, xh, cwbh[:, wc:wc + 512], start=(c == 0), stop=False)
                T.matmul(po, xh, cwbl[:, wc:wc + 512], start=False, stop=False)
                TS.inc(T.matmul(po, xl, cwbh[:, wc:wc + 512], start=False, stop=(c == 7)))
        SC.wait_ge(TS.h, TS.n)
        SS.inc(SC.activation(out=g32[:, 0:512], in_=psO[:, 0:512], func=AF.Silu))
        SS.inc(SC.activation(out=g32[:, 512:1024], in_=psO[:, 512:1024], func=AF.Silu))
        G.wait_ge(SS.h, SS.n)
        ST.inc(G.dma_start(out=cx.gsilu_d[128 * t:128 * (t + 1), :], in_=g32[:]), 16)
        SY.wait_ge(ST.h, ST.n)
        SC.wait_ge(ST.h, ST.n)

    cwq = SB("a1_cwq", [128, 4 * 128])
    cwv = SB("a1_cwv", [128, 2 * 128])
    w32 = SB("a1_w32", [128, 128], F32)
    keffh = SB("a1_keffh", [128, 4096])
    keffl = SB("a1_keffl", [128, 4096])
    vbh = SB("a1_vbh", [128, 4096])
    vbl = SB("a1_vbl", [128, 4096])
    qefh = SB("a1_qefh", [128, 2048])
    qefl = SB("a1_qefl", [128, 2048])
    skh = SB("a1_skh", [128, 512])
    skl = SB("a1_skl", [128, 512])
    ck32 = SB("a1_ck32", [128, 512], F32)
    sk32 = SB("a1_sk32", [128, 512], F32)
    gd32 = SB("a1_gd32", [128, 1024], F32)
    k32 = SB("a1_k32", [128, 512], F32)
    sh32 = SB("a1_sh32", [128, 512], F32)
    t132 = SB("a1_t132", [128, 512], F32)
    ctabs = SB("a1_ctab", [128, 512], F32)
    Phb = SB("a1_Phb", [128, 2 * 512])
    Plb = SB("a1_Plb", [128, 2 * 512])
    msk32 = SB("a1_msk", [128, 128], F32)
    r32b = SB("a1_r32", [128, 128], F32)
    o32q = SB("a1_o32q", [128, 512], F32)
    Ytm = SB("a1_Ytm", [128, 512], F32)
    gnst = SB("a1_gnst", [128, 6], F32)
    gnmv = SB("a1_gnmv", [128, 8], F32)
    gst = SB("a1_gst", [128, 128], F32)
    u32 = SB("a1_u32", [128, 128], F32)
    uh16 = SB("a1_uh16", [128, 128])
    ul16 = SB("a1_ul16", [128, 128])
    u16st = SB("a1_u16st", [128, 256])
    cgnw = SB("a1_cgnw", [128, 1024], F32)
    cgnb = SB("a1_cgnb", [128, 1024], F32)
    cx.barrier()
    LD.inc(SY.dma_start(out=cgnw[:], in_=cx.gnwbc_i[:]), 16)
    LD.inc(SY.dma_start(out=cgnb[:], in_=cx.gnbbc_i[:]), 16)
    for h in range(HEADS):
        cx.barrier()
        for wi, (wsrc, dst0) in enumerate(((cx.wq1, 0), (cx.wk1, 2))):
            SY.wait_ge(VS.h, VS.n)
            LD.inc(SY.dma_start(out=w32[:], in_=wsrc[128 * h:128 * (h + 1), :]), 16)
            SC.wait_ge(LD.h, LD.n)
            SS.inc(SC.copy(out=cwq[:, 128 * dst0:128 * (dst0 + 1)], in_=w32[:]))
            V.wait_ge(SS.h, SS.n)
            VS.inc(V.tensor_tensor(out=cwq[:, 128 * (dst0 + 1):128 * (dst0 + 2)], in0=w32[:],
                                   in1=cwq[:, 128 * dst0:128 * (dst0 + 1)], op=ALU.subtract))
        SY.wait_ge(VS.h, VS.n)
        LD.inc(SY.dma_start(out=w32[:], in_=cx.wv1[128 * h:128 * (h + 1), :]), 16)
        SC.wait_ge(LD.h, LD.n)
        SS.inc(SC.copy(out=cwv[:, 0:128], in_=w32[:]))
        V.wait_ge(SS.h, SS.n)
        VS.inc(V.tensor_tensor(out=cwv[:, 128:256], in0=w32[:], in1=cwv[:, 0:128],
                               op=ALU.subtract))
        LD.inc(SY.dma_start(out=gd32[:, 0:512], in_=cx.gdec[:, 1024 * h:1024 * h + 512]), 16)
        LD.inc(SY.dma_start(out=gd32[:, 512:1024], in_=cx.gdec[:, 1024 * h + 512:1024 * (h + 1)]), 16)
        LD.inc(SY.dma_start(out=ctabs[:, 0:512], in_=cx.ctab_i[:, 512 * h:512 * (h + 1)]), 16)
        for sp in range(8):
            SY.wait_ge(TS.h, TS.n)
            LD.inc(SY.dma_start(out=skh[:], in_=cx.xfm_h_d[128 * h:128 * (h + 1),
                                                           512 * sp:512 * (sp + 1)]), 16)
            LD.inc(SY.dma_start(out=skl[:], in_=cx.xfm_l_d[128 * h:128 * (h + 1),
                                                           512 * sp:512 * (sp + 1)]), 16)
            SY.wait_ge(VS.h, VS.n)
            LD.inc(SY.dma_start(out=ck32[:], in_=cx.cosk[:, 512 * sp:512 * (sp + 1)]), 16)
            LD.inc(SY.dma_start(out=sk32[:], in_=cx.sink[:, 512 * sp:512 * (sp + 1)]), 16)
            T.wait_ge(LD.h, LD.n)
            T.wait_ge(SS.h, SS.n)
            T.matmul(psA[:], cwq[:, 256:384], skh[:], start=True, stop=False)
            T.matmul(psA[:], cwq[:, 256:384], skl[:], start=False, stop=False)
            TS.inc(T.matmul(psA[:], cwq[:, 384:512], skh[:], start=False, stop=True))
            SC.wait_ge(TS.h, TS.n)
            SC.wait_ge(VS.h, VS.n)
            SS.inc(SC.copy(out=k32[:], in_=psA[:]))
            V.wait_ge(SS.h, SS.n)
            VS.inc(V.tensor_tensor(out=t132[:], in0=k32[:], in1=ck32[:], op=ALU.mult))
            VS.inc(V.stream_shuffle(out=sh32[:], in_=k32[:], mask=[p ^ 1 for p in range(32)]))
            V.wait_ge(VS.h, VS.n)
            VS.inc(V.tensor_tensor(out=sh32[:], in0=sh32[:], in1=sk32[:], op=ALU.mult))
            V.wait_ge(VS.h, VS.n)
            VS.inc(V.tensor_tensor(out=t132[:], in0=t132[:], in1=sh32[:], op=ALU.add))
            V.wait_ge(VS.h, VS.n)
            VS.inc(V.tensor_tensor(out=t132[:], in0=t132[:], in1=gd32[:, 512:1024], op=ALU.mult))
            SC.wait_ge(VS.h, VS.n)
            SS.inc(SC.copy(out=keffh[:, 512 * sp:512 * (sp + 1)], in_=t132[:]))
            V.wait_ge(SS.h, SS.n)
            VS.inc(V.tensor_tensor(out=keffl[:, 512 * sp:512 * (sp + 1)], in0=t132[:],
                                   in1=keffh[:, 512 * sp:512 * (sp + 1)], op=ALU.subtract))
            for j in range(4):
                gt = sp * 4 + j
                T.wait_ge(VS.h, VS.n)
                T.wait_ge(SS.h, SS.n)
                T.matmul(psB[:, 0:128], skh[:, 128 * j:128 * (j + 1)], cwv[:, 0:128],
                         start=True, stop=False)
                T.matmul(psB[:, 0:128], skh[:, 128 * j:128 * (j + 1)], cwv[:, 128:256],
                         start=False, stop=False)
                TS.inc(T.matmul(psB[:, 0:128], skl[:, 128 * j:128 * (j + 1)], cwv[:, 0:128],
                                start=False, stop=True))
                SC.wait_ge(TS.h, TS.n)
                SS.inc(SC.copy(out=vbh[:, 128 * gt:128 * (gt + 1)], in_=psB[:, 0:128]))
                V.wait_ge(SS.h, SS.n)
                VS.inc(V.scalar_tensor_tensor(out=vbl[:, 128 * gt:128 * (gt + 1)],
                                              in0=psB[:, 0:128], scalar=1.0,
                                              in1=vbh[:, 128 * gt:128 * (gt + 1)],
                                              op0=ALU.mult, op1=ALU.subtract))
        if h == 0:
            cx.tap("d1_keff", keffh[:, 0:1024])
            cx.tap("d1_vb", vbh[:, 0:512])
        for osp in range(4):
            SY.wait_ge(TS.h, TS.n)
            LD.inc(SY.dma_start(out=skh[:], in_=cx.xfmo_h_d[128 * h:128 * (h + 1),
                                                            512 * osp:512 * (osp + 1)]), 16)
            LD.inc(SY.dma_start(out=skl[:], in_=cx.xfmo_l_d[128 * h:128 * (h + 1),
                                                            512 * osp:512 * (osp + 1)]), 16)
            SY.wait_ge(VS.h, VS.n)
            LD.inc(SY.dma_start(out=ck32[:], in_=cx.cosq[:, 512 * osp:512 * (osp + 1)]), 16)
            LD.inc(SY.dma_start(out=sk32[:], in_=cx.sinq[:, 512 * osp:512 * (osp + 1)]), 16)
            T.wait_ge(LD.h, LD.n)
            T.wait_ge(SS.h, SS.n)
            T.matmul(psA[:], cwq[:, 0:128], skh[:], start=True, stop=False)
            T.matmul(psA[:], cwq[:, 0:128], skl[:], start=False, stop=False)
            TS.inc(T.matmul(psA[:], cwq[:, 128:256], skh[:], start=False, stop=True))
            SC.wait_ge(TS.h, TS.n)
            SC.wait_ge(VS.h, VS.n)
            SS.inc(SC.copy(out=k32[:], in_=psA[:]))
            V.wait_ge(SS.h, SS.n)
            VS.inc(V.tensor_tensor(out=t132[:], in0=k32[:], in1=ck32[:], op=ALU.mult))
            VS.inc(V.stream_shuffle(out=sh32[:], in_=k32[:], mask=[p ^ 1 for p in range(32)]))
            V.wait_ge(VS.h, VS.n)
            VS.inc(V.tensor_tensor(out=sh32[:], in0=sh32[:], in1=sk32[:], op=ALU.mult))
            V.wait_ge(VS.h, VS.n)
            VS.inc(V.tensor_tensor(out=t132[:], in0=t132[:], in1=sh32[:], op=ALU.add))
            V.wait_ge(VS.h, VS.n)
            VS.inc(V.tensor_tensor(out=t132[:], in0=t132[:], in1=gd32[:, 0:512], op=ALU.mult))
            SC.wait_ge(VS.h, VS.n)
            SS.inc(SC.copy(out=qefh[:, 512 * osp:512 * (osp + 1)], in_=t132[:]))
            V.wait_ge(SS.h, SS.n)
            VS.inc(V.tensor_tensor(out=qefl[:, 512 * osp:512 * (osp + 1)], in0=t132[:],
                                   in1=qefh[:, 512 * osp:512 * (osp + 1)], op=ALU.subtract))
        if h == 0:
            cx.tap("d1_qeff", qefh[:, 0:1024])
        vmark = [VS.n, VS.n]
        pmark = [TS.n, TS.n]
        for qi in range(4):
            q0 = 512 * qi
            nk = KEND1[qi]
            for ki in range(nk):
                ktc = ki
                bb = ki % 2
                scps = psA if bb == 0 else psB
                T.wait_ge(VS.h, max(vmark[bb], VS.n if ki == 0 else 0))
                kc = 128 * ktc
                T.matmul(scps[:], keffh[:, kc:kc + 128], qefh[:, q0:q0 + 512],
                         start=True, stop=False)
                T.matmul(scps[:], keffh[:, kc:kc + 128], qefl[:, q0:q0 + 512],
                         start=False, stop=False)
                TS.inc(T.matmul(scps[:], keffl[:, kc:kc + 128], qefh[:, q0:q0 + 512],
                                start=False, stop=True))
                SC.wait_ge(TS.h, TS.n)
                V.wait_ge(TS.h, TS.n)
                V.wait_ge(TS.h, pmark[bb])
                window = (ki >= nk - 8)
                for j in range(4):
                    ccol = ctabs[:, ((qi * 4 + j) * 32 + ktc):((qi * 4 + j) * 32 + ktc) + 1]
                    ph = Phb[:, 512 * bb + 128 * j:512 * bb + 128 * (j + 1)]
                    pl = Plb[:, 512 * bb + 128 * j:512 * bb + 128 * (j + 1)]
                    pss = scps[:, 128 * j:128 * (j + 1)]
                    if window:
                        VS.inc(V.tensor_scalar(out=msk32[:], in0=cx.cqpos[:, q0 + 128 * j:q0 + 128 * (j + 1)],
                                               scalar1=cx.cktpos[:, ktc:ktc + 1], scalar2=ccol,
                                               op0=ALU.is_ge, op1=ALU.mult))
                        V.wait_ge(VS.h, VS.n)
                        VS.inc(V.tensor_tensor(out=r32b[:], in0=pss, in1=msk32[:], op=ALU.mult))
                        SC.wait_ge(VS.h, VS.n)
                        SS.inc(SC.copy(out=ph, in_=r32b[:]))
                        V.wait_ge(SS.h, SS.n)
                        VS.inc(V.tensor_tensor(out=pl, in0=r32b[:], in1=ph, op=ALU.subtract))
                    else:
                        VS.inc(V.tensor_scalar(out=ph, in0=pss, scalar1=ccol, scalar2=0.0,
                                               op0=ALU.mult, op1=ALU.add))
                        V.wait_ge(VS.h, VS.n)
                        VS.inc(V.scalar_tensor_tensor(out=pl, in0=pss, scalar=ccol, in1=ph,
                                                      op0=ALU.mult, op1=ALU.subtract))
                vmark[bb] = VS.n
                if h == 0 and qi == 3 and ki in (0, 16, 26, 28):
                    cx.tap({0: "d1_Ph", 16: "d1_PhA", 26: "d1_PhB", 28: "d1_PhC"}[ki],
                           Phb[:, 512 * bb:512 * (bb + 1)])
                    cx.tap({0: "d1_Pl", 16: "d1_PlA", 26: "d1_PlB", 28: "d1_PlC"}[ki],
                           Plb[:, 512 * bb:512 * (bb + 1)])
                T.wait_ge(VS.h, VS.n)
                T.wait_ge(SS.h, SS.n)
                vc = 128 * ktc
                T.matmul(psO[:, 0:512], vbh[:, vc:vc + 128], Phb[:, 512 * bb:512 * (bb + 1)],
                         start=(ki == 0), stop=False)
                T.matmul(psO[:, 0:512], vbh[:, vc:vc + 128], Plb[:, 512 * bb:512 * (bb + 1)],
                         start=False, stop=False)
                TS.inc(T.matmul(psO[:, 0:512], vbl[:, vc:vc + 128], Phb[:, 512 * bb:512 * (bb + 1)],
                                start=False, stop=(ki == nk - 1)))
                pmark[bb] = TS.n
            SC.wait_ge(TS.h, TS.n)
            SS.inc(SC.copy(out=o32q[:], in_=psO[:, 0:512]))
            if h == 0 and qi == 3:
                cx.tap("d1_Y", o32q[:])
            T.wait_ge(SS.h, SS.n)
            for j in range(4):
                T.wait_ge(VS.h, VS.n)
                TS.inc(T.transpose(out=psB[0:128, 0:128], in_=o32q[:, 128 * j:128 * (j + 1)],
                                   identity=cx.cid32[:]))
                V.wait_ge(TS.h, TS.n)
                VS.inc(V.tensor_copy(out=Ytm[:, 128 * j:128 * (j + 1)], in_=psB[0:128, 0:128]))
            for j in range(4):
                t = qi * 4 + j
                ysl = Ytm[:, 128 * j:128 * (j + 1)]
                V.wait_ge(VS.h, VS.n)
                VS.inc(V.bn_stats(out=gnst[:], in_=ysl))
                V.wait_ge(VS.h, VS.n)
                VS.inc(V.bn_aggr(out=gnmv[:, 0:2], in_=gnst[:]))
                V.wait_ge(VS.h, VS.n)
                VS.inc(V.tensor_scalar(out=gnmv[:, 2:3], in0=gnmv[:, 0:1], scalar1=-1.0,
                                       scalar2=0.0, op0=ALU.mult, op1=ALU.add))
                VS.inc(V.tensor_scalar(out=gnmv[:, 3:4], in0=gnmv[:, 1:2], scalar1=1.0,
                                       scalar2=1e-5, op0=ALU.mult, op1=ALU.add))
                V.wait_ge(VS.h, VS.n)
                VS.inc(V.reciprocal(out=gnmv[:, 4:5], in_=gnmv[:, 3:4]))
                SC.wait_ge(VS.h, VS.n)
                SS.inc(SC.activation(out=gnmv[:, 5:6], in_=gnmv[:, 4:5], func=AF.Sqrt))
                SY.wait_ge(VS.h, VS.n)
                LD.inc(SY.dma_start(out=gst[:], in_=cx.gsilu_d[128 * t:128 * (t + 1),
                                                              128 * h:128 * (h + 1)]), 16)
                V.wait_ge(SS.h, SS.n)
                V.wait_ge(LD.h, LD.n)
                VS.inc(V.tensor_scalar(out=u32[:], in0=ysl, scalar1=gnmv[:, 2:3],
                                       scalar2=gnmv[:, 5:6], op0=ALU.add, op1=ALU.mult))
                V.wait_ge(VS.h, VS.n)
                VS.inc(V.tensor_tensor(out=u32[:], in0=u32[:], in1=cgnw[:, 128 * h:128 * (h + 1)],
                                       op=ALU.mult))
                V.wait_ge(VS.h, VS.n)
                VS.inc(V.tensor_tensor(out=u32[:], in0=u32[:], in1=cgnb[:, 128 * h:128 * (h + 1)],
                                       op=ALU.add))
                V.wait_ge(VS.h, VS.n)
                VS.inc(V.tensor_tensor(out=u32[:], in0=u32[:], in1=gst[:], op=ALU.mult))
                if h == 0 and qi == 3 and j == 0:
                    cx.tap("d1_gs", gst[:])
                    cx.tap("d1_u", u32[:])
                SC.wait_ge(VS.h, VS.n)
                SS.inc(SC.copy(out=uh16[:], in_=u32[:]))
                V.wait_ge(SS.h, SS.n)
                VS.inc(V.tensor_tensor(out=ul16[:], in0=u32[:], in1=uh16[:], op=ALU.subtract))
                T.wait_ge(VS.h, VS.n)
                for hl, srcb in enumerate((uh16, ul16)):
                    pt = psE16 if hl == 0 else psF16
                    TS.inc(T.transpose(out=pt[0:128, 0:128], in_=srcb[:], identity=cx.cid16[:]))
                    V.wait_ge(TS.h, TS.n)
                    VS.inc(V.tensor_copy(out=u16st[:, 128 * hl:128 * (hl + 1)],
                                         in_=pt[0:128, 0:128]))
                G.wait_ge(VS.h, VS.n)
                ST.inc(G.dma_start(out=cx.ufm_h_d[128 * h:128 * (h + 1), 128 * t:128 * (t + 1)],
                                   in_=u16st[:, 0:128]), 16)
                ST.inc(G.dma_start(out=cx.ufm_l_d[128 * h:128 * (h + 1), 128 * t:128 * (t + 1)],
                                   in_=u16st[:, 128:256]), 16)
                V.wait_ge(ST.h, ST.n)

    cx.barrier()
    VS.inc(V.memset(cx.accln[:], 0.0))
    for c in range(8):
        for half in range(2):
            SY.wait_ge(VS.h, VS.n)
            LD.inc(SY.dma_start(out=stage32[:],
                                in_=cx.wo_l1[128 * c:128 * (c + 1), 512 * half:512 * (half + 1)]), 16)
            wc = 1024 * c + 512 * half
            SC.wait_ge(LD.h, LD.n)
            SS.inc(SC.copy(out=cwbh[:, wc:wc + 512], in_=stage32[:]))
            V.wait_ge(SS.h, SS.n)
            VS.inc(V.tensor_tensor(out=cwbl[:, wc:wc + 512], in0=stage32[:],
                                   in1=cwbh[:, wc:wc + 512], op=ALU.subtract))
    for t in range(NT):
        SY.wait_ge(TS.h, TS.n)
        for c in range(8):
            LD.inc(SY.dma_start(out=sfh[:, 128 * c:128 * (c + 1)],
                                in_=cx.ufm_h_d[128 * c:128 * (c + 1), 128 * t:128 * (t + 1)]), 16)
            LD.inc(SY.dma_start(out=sfl[:, 128 * c:128 * (c + 1)],
                                in_=cx.ufm_l_d[128 * c:128 * (c + 1), 128 * t:128 * (t + 1)]), 16)
        T.wait_ge(LD.h, LD.n)
        T.wait_ge(VS.h, VS.n)
        for half in range(2):
            po = psO[:, 512 * half:512 * (half + 1)]
            for c in range(8):
                xh = sfh[:, 128 * c:128 * (c + 1)]
                xl = sfl[:, 128 * c:128 * (c + 1)]
                wc = 1024 * c + 512 * half
                T.matmul(po, xh, cwbh[:, wc:wc + 512], start=(c == 0), stop=False)
                T.matmul(po, xh, cwbl[:, wc:wc + 512], start=False, stop=False)
                TS.inc(T.matmul(po, xl, cwbh[:, wc:wc + 512], start=False, stop=(c == 7)))
        V.wait_ge(TS.h, TS.n)
        xs = xstream[:, 1024 * t:1024 * (t + 1)]
        for half in range(2):
            VS.inc(V.scalar_tensor_tensor(out=xs[:, 512 * half:512 * (half + 1)],
                                          in0=psO[:, 512 * half:512 * (half + 1)], scalar=1.0,
                                          in1=xs[:, 512 * half:512 * (half + 1)],
                                          op0=ALU.mult, op1=ALU.add,
                                          accum_out=cx.accln[:, 8 * t + half:8 * t + half + 1]))
    cx.tap("d1_vst", xstream[:, 0:1024])
    cx.barrier()
    if do_ln:
        esl = ExitStack()
        _ln_round(cx, esl, "a1")
        cx.barrier()
        esl.close()
    es.close()


NSLOT_ = 23


def _l1ffn(cx):
    nc = cx.nc
    T, V, SC, SY, G = cx.T, cx.V, cx.SC, cx.SY, cx.G
    LD, ST, TS, SS, VS = cx.LD, cx.ST, cx.TS, cx.SS, cx.VS
    psA, psB, psO = cx.psA, cx.psB, cx.psO
    psE16, psF16 = cx.psE16, cx.psF16
    xstream = cx.xstream
    es = ExitStack()
    SB = lambda n, sh, dt=F32: es.enter_context(nc.sbuf_tensor(n, sh, dt)).ap()

    if DUMP2:
        cx.dstash = nc.alloc_sbuf_tensor("f1_dstash", [128, 128], F32).ap()
    xfmh = SB("f1_xfmh", [128, 1024], F16)
    xfml = SB("f1_xfml", [128, 1024], F16)
    x1h = SB("f1_x1h", [128, 1024], F16)
    x1l = SB("f1_x1l", [128, 1024], F16)
    crwh = SB("f1_crwh", [128, 8 * 8], F16)
    crwl = SB("f1_crwl", [128, 8 * 8], F16)
    stg = SB("f1_stg", [128, 8])
    lg = SB("f1_lg", [128, 8])
    gsc = SB("f1_gsc", [128, 8])
    ohall = SB("f1_ohall", [128, 16 * 8], F16)
    prefall = SB("f1_prefall", [128, 16 * 8])
    wtcol = SB("f1_wtcol", [128, 16])
    destu = SB("f1_destu", [128, 16], U32)
    tri128 = SB("f1_tri128", [128, 128], F16)
    tri16 = SB("f1_tri16", [16, 16], F16)
    tri8 = SB("f1_tri8", [8, 8], F16)
    slotbc = SB("f1_slotbc", [128, 8])
    o1x128f16 = SB("f1_o1x128f16", [1, 128], F16)
    tcnt = SB("f1_tcnt", [8, 16])
    cthr = SB("f1_cthr", [8, 16])
    indt = SB("f1_indt", [8, 16])
    smal = SB("f1_smal", [8, 4])
    pc16 = SB("f1_pc16", [8, 1], F16)
    tcT16 = SB("f1_tcT16", [16, 16], F16)
    tbT = SB("f1_tbT", [16, 8])
    rowsml = SB("f1_rowsml", [1, 8])
    bexbc = SB("f1_bexbc", [128, 8])
    bincbc = SB("f1_bincbc", [128, 8])
    eidc = SB("f1_eidc", [128, 1])
    eidi = SB("f1_eidi", [128, 1], I32)
    eidsav = SB("f1_eidsav", [128, 1])
    tbbc = SB("f1_tbbc", [128, 8])
    s3 = SB("f1_s3", [128, 8])

    cx.barrier()
    VS.inc(V.memset(cx.accln[:], 0.0))
    for dst, srcd in ((tri128, cx.tri128su), (tri16, cx.tri16su), (tri8, cx.tri8su),
                      (slotbc, cx.slotbc_i), (o1x128f16, cx.ones1x128f16),
                      (cthr, cx.thr16)):
        LD.inc(SY.dma_start(out=dst[:], in_=srcd[:]), 16)
    for c in range(8):
        SY.wait_ge(VS.h, VS.n)
        LD.inc(SY.dma_start(out=stg[:], in_=cx.router8[128 * c:128 * (c + 1), :]), 16)
        SC.wait_ge(LD.h, LD.n)
        SS.inc(SC.copy(out=crwh[:, 8 * c:8 * (c + 1)], in_=stg[:]))
        V.wait_ge(SS.h, SS.n)
        VS.inc(V.tensor_tensor(out=crwl[:, 8 * c:8 * (c + 1)], in0=stg[:],
                               in1=crwh[:, 8 * c:8 * (c + 1)], op=ALU.subtract))
    for t in range(NT):
        xs = xstream[:, 1024 * t:1024 * (t + 1)]
        bm = [VS.n, VS.n]
        _split_transpose_tile(cx, xs, xfmh, xfml, lambda c: 128 * c, x1h, x1l, bm)
        T.wait_ge(VS.h, VS.n)
        T.wait_ge(SS.h, SS.n)
        for c in range(8):
            xh = xfmh[:, 128 * c:128 * (c + 1)]
            xl = xfml[:, 128 * c:128 * (c + 1)]
            T.matmul(psB[:, 0:8], xh, crwh[:, 8 * c:8 * (c + 1)], start=(c == 0), stop=False)
            T.matmul(psB[:, 0:8], xh, crwl[:, 8 * c:8 * (c + 1)], start=False, stop=False)
            TS.inc(T.matmul(psB[:, 0:8], xl, crwh[:, 8 * c:8 * (c + 1)], start=False,
                            stop=(c == 7)))
        SC.wait_ge(TS.h, TS.n)
        SC.wait_ge(VS.h, VS.n)
        SS.inc(SC.copy(out=lg[:], in_=psB[:, 0:8]))
        V.wait_ge(SS.h, SS.n)
        VS.inc(V.reduce_max(out=gsc[:, 0:1], in_=lg[:], axis=AX.X))
        V.wait_ge(VS.h, VS.n)
        VS.inc(V.tensor_scalar(out=gsc[:, 1:2], in0=gsc[:, 0:1], scalar1=-1.0, scalar2=0.0,
                               op0=ALU.mult, op1=ALU.add))
        SC.wait_ge(VS.h, VS.n)
        SS.inc(SC.activation(out=stg[:, 0:8], in_=lg[:], func=AF.Exp, bias=gsc[:, 1:2],
                             scale=1.0, accum_out=wtcol[:, t:t + 1]))
        V.wait_ge(SS.h, SS.n)
        VS.inc(V.tensor_scalar(out=ohall[:, 8 * t:8 * (t + 1)], in0=lg[:], scalar1=gsc[:, 1:2],
                               scalar2=0.0, op0=ALU.add, op1=ALU.is_equal))
        T.wait_ge(VS.h, VS.n)
        TS.inc(T.matmul(psB[:, 8:16], tri128[:], ohall[:, 8 * t:8 * (t + 1)],
                        start=True, stop=True))
        TS.inc(T.matmul(psA[0:8, t:t + 1], ohall[:, 8 * t:8 * (t + 1)], cx.cones16[:],
                        start=(t == 0), stop=(t == NT - 1)))
        SC.wait_ge(TS.h, TS.n)
        SS.inc(SC.copy(out=prefall[:, 8 * t:8 * (t + 1)], in_=psB[:, 8:16]))
        if DUMP2 and t == 0:
            V.wait_ge(SS.h, SS.n)
            VS.inc(V.tensor_copy(out=cx.dstash[:, 0:8], in_=lg[:]))
            VS.inc(V.tensor_copy(out=cx.dstash[:, 61:69], in_=ohall[:, 0:8]))
    if F1STAGE < 2:
        cx.barrier()
        es.close()
        return
    V.wait_ge(SS.h, SS.n)
    VS.inc(V.reciprocal(out=wtcol[:], in_=wtcol[:]))
    SC.wait_ge(TS.h, TS.n)
    SS.inc(SC.copy(out=tcnt[:], in_=psA[0:8, 0:16]))
    V.wait_ge(SS.h, SS.n)
    VS.inc(V.reduce_sum(out=smal[:, 0:1], in_=tcnt[:], axis=AX.X))
    V.wait_ge(VS.h, VS.n)
    VS.inc(V.tensor_scalar(out=indt[:], in0=cthr[:], scalar1=smal[:, 0:1], scalar2=0.0,
                           op0=ALU.is_le, op1=ALU.add))
    V.wait_ge(VS.h, VS.n)
    VS.inc(V.reduce_sum(out=smal[:, 1:2], in_=indt[:], axis=AX.X))
    V.wait_ge(VS.h, VS.n)
    VS.inc(V.tensor_scalar(out=smal[:, 2:3], in0=smal[:, 1:2], scalar1=128.0, scalar2=0.0,
                           op0=ALU.mult, op1=ALU.add))
    SC.wait_ge(VS.h, VS.n)
    SS.inc(SC.copy(out=pc16[:], in_=smal[:, 2:3]))
    T.wait_ge(SS.h, SS.n)
    TS.inc(T.matmul(psB[0:8, 16:17], tri8[:], pc16[:], start=True, stop=True))
    SC.wait_ge(TS.h, TS.n)
    SS.inc(SC.copy(out=smal[:, 3:4], in_=psB[0:8, 16:17]))
    V.wait_ge(SS.h, SS.n)
    VS.inc(V.tensor_copy(out=tcT16[0:8, 0:16], in_=tcnt[:]))
    T.wait_ge(VS.h, VS.n)
    TS.inc(T.transpose(out=psE16[0:16, 0:8], in_=tcT16[0:8, 0:16], identity=cx.cid16[0:8, 0:8]))
    V.wait_ge(TS.h, TS.n)
    VS.inc(V.tensor_copy(out=tcT16[0:16, 0:8], in_=psE16[0:16, 0:8]))
    T.wait_ge(VS.h, VS.n)
    TS.inc(T.matmul(psB[0:16, 17:25], tri16[:], tcT16[0:16, 0:8], start=True, stop=True))
    SC.wait_ge(TS.h, TS.n)
    SS.inc(SC.copy(out=tbT[:], in_=psB[0:16, 17:25]))
    G.wait_ge(SS.h, SS.n)
    ST.inc(G.dma_start(out=cx.tb_d[:, :], in_=tbT[:]), 16)
    V.wait_ge(SS.h, SS.n)
    VS.inc(V.tensor_tensor(out=smal[:, 1:2], in0=smal[:, 3:4], in1=smal[:, 2:3], op=ALU.add))
    SC.wait_ge(VS.h, VS.n)
    SS.inc(SC.copy(out=pc16[:], in_=smal[:, 1:2]))
    T.wait_ge(SS.h, SS.n)
    TS.inc(T.transpose(out=psE16[0:1, 0:8], in_=pc16[:], identity=cx.cid16[0:8, 0:8]))
    V.wait_ge(TS.h, TS.n)
    VS.inc(V.tensor_copy(out=rowsml[:], in_=psE16[0:1, 0:8]))
    T.wait_ge(VS.h, VS.n)
    TS.inc(T.matmul(psB[:, 25:33], cx.cones1x128[:], rowsml[:], start=True, stop=True))
    SC.wait_ge(TS.h, TS.n)
    SS.inc(SC.copy(out=bincbc[:], in_=psB[:, 25:33]))
    V.wait_ge(SS.h, SS.n)
    SC.wait_ge(VS.h, VS.n)
    SS.inc(SC.copy(out=pc16[:], in_=smal[:, 3:4]))
    T.wait_ge(SS.h, SS.n)
    TS.inc(T.transpose(out=psE16[0:1, 0:8], in_=pc16[:], identity=cx.cid16[0:8, 0:8]))
    V.wait_ge(TS.h, TS.n)
    VS.inc(V.tensor_copy(out=rowsml[:], in_=psE16[0:1, 0:8]))
    T.wait_ge(VS.h, VS.n)
    TS.inc(T.matmul(psB[:, 33:41], cx.cones1x128[:], rowsml[:], start=True, stop=True))
    SC.wait_ge(TS.h, TS.n)
    SS.inc(SC.copy(out=bexbc[:], in_=psB[:, 33:41]))
    V.wait_ge(SS.h, SS.n)
    VS.inc(V.tensor_tensor(out=s3[:], in0=slotbc[:], in1=bincbc[:], op=ALU.is_ge))
    V.wait_ge(VS.h, VS.n)
    VS.inc(V.reduce_sum(out=eidc[:, 0:1], in_=s3[:], axis=AX.X))
    V.wait_ge(VS.h, VS.n)
    VS.inc(V.tensor_scalar(out=eidc[:, 0:1], in0=eidc[:, 0:1], scalar1=7.0, scalar2=0.0,
                           op0=ALU.min, op1=ALU.max))
    V.wait_ge(VS.h, VS.n)
    VS.inc(V.tensor_copy(out=eidi[:], in_=eidc[:, 0:1]))
    VS.inc(V.tensor_copy(out=eidsav[:], in_=eidc[:, 0:1]))
    G.wait_ge(VS.h, VS.n)
    ST.inc(G.dma_start(out=cx.eid_d[:, :], in_=eidi[:]), 16)
    if DUMP2:
        V.wait_ge(VS.h, VS.n)
        VS.inc(V.tensor_copy(out=cx.dstash[0:8, 8:24], in_=tcnt[:]))
        VS.inc(V.tensor_copy(out=cx.dstash[0:8, 24:28], in_=smal[:]))
        VS.inc(V.tensor_copy(out=cx.dstash[:, 28:29], in_=eidc[:, 0:1]))
        VS.inc(V.tensor_copy(out=cx.dstash[:, 29:45], in_=wtcol[:]))
        VS.inc(V.tensor_copy(out=cx.dstash[0:16, 45:53], in_=tbT[:]))
        VS.inc(V.tensor_copy(out=cx.dstash[:, 53:61], in_=bincbc[:]))
    cx.tap("dF_wt", wtcol[:])
    cx.tap("dF_eid", eidc[:, 0:1])
    cx.tap("dF_tcnt", tcnt[:])
    cx.tap("dF_smal", smal[:])
    cx.tap("dF_oh", ohall[:])
    if F1STAGE < 3:
        cx.barrier()
        es.close()
        return
    for t in range(NT):
        SY.wait_ge(VS.h, VS.n)
        LD.inc(SY.dma_start(out=rowsml[:], in_=cx.tb_d[t:t + 1, :]), 16)
        T.wait_ge(LD.h, LD.n)
        T.wait_ge(SS.h, SS.n)
        TS.inc(T.matmul(psB[:, 41:49], cx.cones1x128[:], rowsml[:], start=True, stop=True))
        SC.wait_ge(TS.h, TS.n)
        SS.inc(SC.copy(out=tbbc[:], in_=psB[:, 41:49]))
        V.wait_ge(SS.h, SS.n)
        VS.inc(V.tensor_tensor(out=s3[:], in0=prefall[:, 8 * t:8 * (t + 1)], in1=bexbc[:],
                               op=ALU.add))
        V.wait_ge(VS.h, VS.n)
        VS.inc(V.tensor_tensor(out=s3[:], in0=s3[:], in1=tbbc[:], op=ALU.add))
        V.wait_ge(VS.h, VS.n)
        VS.inc(V.tensor_tensor(out=s3[:], in0=s3[:], in1=ohall[:, 8 * t:8 * (t + 1)],
                               op=ALU.mult))
        V.wait_ge(VS.h, VS.n)
        VS.inc(V.reduce_sum(out=eidc[:, 0:1], in_=s3[:], axis=AX.X))
        V.wait_ge(VS.h, VS.n)
        VS.inc(V.tensor_scalar(out=eidc[:, 0:1], in0=eidc[:, 0:1], scalar1=2943.0, scalar2=0.0,
                               op0=ALU.min, op1=ALU.max))
        V.wait_ge(VS.h, VS.n)
        VS.inc(V.tensor_copy(out=destu[:, t:t + 1], in_=eidc[:, 0:1]))
        G.wait_ge(VS.h, VS.n)
        ST.inc(G.indirect_dma_start(
            out=cx.xsort_d[:, :], out_offset=IndirectOffsetOnAxis(ap=destu[:, t:t + 1], axis=0),
            in_=xstream[:, 1024 * t:1024 * (t + 1)], in_offset=None), 16)
    G.wait_ge(ST.h, ST.n)
    if DUMP2:
        V.wait_ge(ST.h, ST.n)
        VS.inc(V.tensor_copy(out=cx.dstash[:, 69:85], in_=destu[:]))
    if F1STAGE < 4:
        cx.barrier()
        es.close()
        return
    xs32 = SB("f1_xs32", [128, 1024])
    xfm32 = SB("f1_xfm32", [128, 1024])
    w1blk = SB("f1_w1blk", [128, 8 * 512])
    w2blk = SB("f1_w2blk", [128, 4 * 1024])
    b1bc = SB("f1_b1bc", [128, 512])
    b2bc = SB("f1_b2bc", [128, 1024])
    h32 = SB("f1_h32", [128, 512])
    hfm32 = SB("f1_hfm32", [128, 32 * 128])
    y32 = SB("f1_y32", [128, 1024])
    rowpat = SB("f1_rowpat", [128, 8])
    e1024 = SB("f1_e1024", [128, 32])
    iWu = SB("f1_iWu", [128, 8], U32)
    iWf = SB("f1_iWf", [128, 8])
    ebu = SB("f1_ebu", [128, 32], U32)
    eidbc = SB("f1_eidbc", [128, 128])
    LD.inc(SY.dma_start(out=rowpat[:], in_=cx.rowpatW[:]), 16)
    eid16c = SB("f1_eid16c", [128, 1], F16)
    V.wait_ge(VS.h, VS.n)
    VS.inc(V.tensor_copy(out=eid16c[:], in_=eidsav[:, 0:1]))
    T.wait_ge(VS.h, VS.n)
    TS.inc(T.transpose(out=psE16[0:1, 0:128], in_=eid16c[:], identity=cx.cid16[:]))
    V.wait_ge(TS.h, TS.n)
    VS.inc(V.tensor_copy(out=eidbc[0:1, 0:128], in_=psE16[0:1, 0:128]))
    T.wait_ge(VS.h, VS.n)
    TS.inc(T.matmul(psO[:, 0:128], cx.cones1x128[:],
                    eidbc[0:1, 0:128], start=True, stop=True))
    SC.wait_ge(TS.h, TS.n)
    SS.inc(SC.copy(out=eidbc[:], in_=psO[:, 0:128]))
    V.wait_ge(SS.h, SS.n)
    VS.inc(V.tensor_scalar(out=eidbc[:, 0:23], in0=eidbc[:, 0:23], scalar1=7.0, scalar2=0.0,
                           op0=ALU.min, op1=ALU.max))
    V.wait_ge(VS.h, VS.n)
    VS.inc(V.tensor_scalar(out=e1024[:, 0:23], in0=eidbc[:, 0:23], scalar1=1024.0, scalar2=0.0,
                           op0=ALU.mult, op1=ALU.add))
    VS.inc(V.tensor_copy(out=ebu[:, 0:23], in_=eidbc[:, 0:23]))
    for s in range(NSLOT_):
        SY.wait_ge(TS.h, TS.n)
        LD.inc(SY.dma_start(out=xs32[:], in_=cx.xsort_d[128 * s:128 * (s + 1), :]), 16)
        V.wait_ge(VS.h, VS.n)
        VS.inc(V.tensor_scalar(out=iWf[:], in0=rowpat[:], scalar1=e1024[:, s:s + 1],
                               scalar2=0.0, op0=ALU.add, op1=ALU.add))
        V.wait_ge(VS.h, VS.n)
        VS.inc(V.tensor_scalar(out=iWf[:], in0=iWf[:], scalar1=8191.0, scalar2=0.0,
                               op0=ALU.min, op1=ALU.max))
        V.wait_ge(VS.h, VS.n)
        VS.inc(V.tensor_copy(out=iWu[:], in_=iWf[:]))
        T.wait_ge(LD.h, LD.n)
        T.wait_ge(SS.h, SS.n)
        for c in range(8):
            pt = psA if c % 2 == 0 else psB
            T.wait_ge(VS.h, VS.n)
            TS.inc(T.transpose(out=pt[0:128, 0:128], in_=xs32[:, 128 * c:128 * (c + 1)],
                               identity=cx.cid32[:]))
            V.wait_ge(TS.h, TS.n)
            VS.inc(V.tensor_copy(out=xfm32[:, 128 * c:128 * (c + 1)], in_=pt[0:128, 0:128]))
        G.wait_ge(VS.h, VS.n)
        for ffb in range(8):
            G.wait_ge(TS.h, TS.n)
            if F1SLOT >= 2:
                ST.inc(G.indirect_dma_start(
                    out=w1blk[:, :], out_offset=None, in_=cx.w1g[:, :],
                    in_offset=IndirectOffsetOnAxis(ap=iWu[:, ffb:ffb + 1], axis=0)), 16)
            if F1SLOT >= 3:
                ST.inc(G.indirect_dma_start(
                    out=b1bc[:, :], out_offset=None, in_=cx.b1f[:, :],
                    in_offset=IndirectOffsetOnAxis(ap=ebu[:, s:s + 1], axis=0),
                    element_offset=512 * ffb), 16)
            T.wait_ge(ST.h, ST.n)
            if F1SLOT >= 4:
                for c in range(8):
                    TS.inc(T.matmul(psO[:, 0:512], xfm32[:, 128 * c:128 * (c + 1)],
                                    w1blk[:, 512 * c:512 * (c + 1)], start=(c == 0), stop=(c == 7)))
                V.wait_ge(TS.h, TS.n)
                VS.inc(V.tensor_tensor(out=h32[:], in0=psO[:, 0:512], in1=b1bc[:], op=ALU.add))
                SC.wait_ge(VS.h, VS.n)
                SS.inc(SC.activation(out=h32[:], in_=h32[:], func=AF.Gelu))
                if DUMP2 and s == 0 and ffb == 0:
                    V.wait_ge(SS.h, SS.n)
                    VS.inc(V.tensor_copy(out=cx.dstash[:, 85:117], in_=h32[:, 0:32]))
            else:
                V.wait_ge(ST.h, ST.n)
                VS.inc(V.memset(h32[:], 0.0))
            T.wait_ge(SS.h, SS.n)
            if F1SLOT >= 5:
                for c4 in range(4):
                    pt = psA if c4 % 2 == 0 else psB
                    T.wait_ge(VS.h, VS.n)
                    TS.inc(T.transpose(out=pt[0:128, 0:128], in_=h32[:, 128 * c4:128 * (c4 + 1)],
                                       identity=cx.cid32[:]))
                    V.wait_ge(TS.h, TS.n)
                    VS.inc(V.tensor_copy(out=hfm32[:, 128 * (4 * ffb + c4):128 * (4 * ffb + c4 + 1)],
                                         in_=pt[0:128, 0:128]))
        for ffb in range(8 if F1SLOT >= 6 else 0):
            G.wait_ge(TS.h, TS.n)
            ST.inc(G.indirect_dma_start(
                out=w2blk[:, :], out_offset=None, in_=cx.w2g[:, :],
                in_offset=IndirectOffsetOnAxis(ap=iWu[:, ffb:ffb + 1], axis=0)), 16)
            T.wait_ge(ST.h, ST.n)
            T.wait_ge(VS.h, VS.n)
            for c4 in range(4):
                lh = hfm32[:, 128 * (4 * ffb + c4):128 * (4 * ffb + c4 + 1)]
                for half in range(2):
                    TS.inc(T.matmul(
                        psO[:, 512 * half:512 * (half + 1)], lh,
                        w2blk[:, 1024 * c4 + 512 * half:1024 * c4 + 512 * (half + 1)],
                        start=(ffb == 0 and c4 == 0), stop=(ffb == 7 and c4 == 3)))
        if F1SLOT >= 6:
            ST.inc(G.indirect_dma_start(
                out=b2bc[:, :], out_offset=None, in_=cx.b2f[:, :],
                in_offset=IndirectOffsetOnAxis(ap=ebu[:, s:s + 1], axis=0)), 16)
            V.wait_ge(TS.h, TS.n)
            V.wait_ge(ST.h, ST.n)
            VS.inc(V.tensor_tensor(out=y32[:, 0:512], in0=psO[:, 0:512], in1=b2bc[:, 0:512],
                                   op=ALU.add))
            VS.inc(V.tensor_tensor(out=y32[:, 512:1024], in0=psO[:, 512:1024],
                                   in1=b2bc[:, 512:1024], op=ALU.add))
        else:
            V.wait_ge(ST.h, ST.n)
            VS.inc(V.memset(y32[:], 0.0))
        if s == 0:
            cx.tap("dF_xs0", xs32[:])
            cx.tap("dF_h0", h32[:])
            cx.tap("dF_y0", y32[:])
        if DUMP2 and s == 0:
            V.wait_ge(VS.h, VS.n)
            VS.inc(V.tensor_copy(out=cx.dstash[:, 117:125], in_=y32[:, 0:8]))
        G.wait_ge(VS.h, VS.n)
        ST.inc(G.dma_start(out=cx.ysort_d[128 * s:128 * (s + 1), :], in_=y32[:]), 16)
        V.wait_ge(ST.h, ST.n)
    if F1STAGE < 5:
        cx.barrier()
        es.close()
        return
    cx.barrier()
    for t in range(NT):
        G.wait_ge(VS.h, VS.n)
        LD.inc(G.indirect_dma_start(
            out=xs32[:], out_offset=None,
            in_=cx.ysort_d[:, :], in_offset=IndirectOffsetOnAxis(ap=destu[:, t:t + 1], axis=0)), 16)
        V.wait_ge(LD.h, LD.n)
        xs = xstream[:, 1024 * t:1024 * (t + 1)]
        VS.inc(V.scalar_tensor_tensor(out=xs, in0=xs32[:], scalar=wtcol[:, t:t + 1], in1=xs,
                                      op0=ALU.mult, op1=ALU.add,
                                      accum_out=cx.accln[:, 8 * t:8 * t + 1]))
    cx.tap("dF_vpre", xstream[:, 0:1024])
    cx.barrier()
    esl = ExitStack()
    _ln_round(cx, esl, "f1")
    cx.barrier()
    esl.close()
    es.close()
